# version 19
# baseline (speedup 1.0000x reference)
import contextlib
import numpy as np

import concourse.bass as bass
import concourse.mybir as mybir
from concourse.bass_utils import run_bass_kernel_spmd

F32 = mybir.dt.float32
BF16 = mybir.dt.bfloat16
AF = mybir.ActivationFunctionType
ALU = mybir.AluOpType
AX = mybir.AxisListType

L = 4096
C = 1024
H = 16
D = 64
NT = L // 128
NP = H // 2
SQT = 512
NST = L // SQT
RMS_EPS = 1e-6
LN_EPS = 1e-5
ISQ2 = 0.7071067811865476

ENGINES = ("sync", "tensor", "vector", "scalar", "gpsimd")


def mk(method, *args, **kw):
    def f(eng):
        return getattr(eng, method)(*args, **kw)
    return f


class Plan:
    def __init__(self, nc, sems):
        self.nc = nc
        self.sems = sems
        self.cnt = {k: 0 for k in sems}
        self.ops = {e: [] for e in ENGINES}
        self.chain = {}

    def op(self, eng, fn, wait=(), inc=()):
        for s, n in inc:
            self.cnt[s] += n
        self.ops[eng].append([list(wait), fn, list(inc)])

    def val(self, s):
        return self.cnt[s]

    def after(self, eng, s, n=1):
        assert self.ops[eng], f"no op on {eng}"
        self.ops[eng][-1][2].append((s, n))
        self.cnt[s] += n
        return self.cnt[s]

    CHAIN = {"vector": "sDV", "scalar": "sSC", "gpsimd": "sGP"}

    def sop(self, eng, fn, wait=()):
        cs = self.CHAIN[eng]
        w = list(wait)
        if self.chain.get(eng):
            w.append((cs, self.chain[eng]))
        self.op(eng, fn, wait=w)
        self.chain[eng] = self.after(eng, cs)
        return self.chain[eng]

    def barrier(self):
        dma_waits = [(s, self.cnt[s]) for s in
                     ("sIN", "sWL", "sX", "sQL", "sXL", "sSP", "sMN", "sST",
                      "sOUT", "sQP") if self.cnt[s] > 0]
        self.op("sync", mk("nop"), wait=dma_waits, inc=(("sB", 1),))
        self.op("gpsimd", mk("engine_nop"), wait=dma_waits, inc=(("sB", 1),))
        self.op("vector", mk("engine_nop"), inc=(("sB", 1),))
        self.op("scalar", mk("copy", self.bar_sb[:, 1:2], self.bar_sb[:, 0:1]),
                wait=list(self.bar_wait), inc=(("sB", 1),))
        stage1 = self.cnt["sB"]
        self.op("tensor", mk("transpose", self.bar_ps, self.bar_in,
                             self.bar_in),
                wait=list(self.bar_wait) + [("sB", stage1)],
                inc=(("sB", 1),))
        tgt = self.cnt["sB"]
        for e in ENGINES:
            self.ops[e].append([[("sB", tgt)], None, []])

    def flush(self):
        with self.nc.Block() as block:
            for ename in ENGINES:
                steps = self.ops[ename]
                if not steps:
                    continue

                def section(eng, steps=steps):
                    for waits, fn, incs in steps:
                        for s, v in waits:
                            if v > 0:
                                eng.wait_ge(self.sems[s], v)
                        if fn is None:
                            continue
                        ins = fn(eng)
                        for s, n in incs:
                            ins = ins.then_inc(self.sems[s], n)

                getattr(block, ename)(section)
        self.ops = {e: [] for e in ENGINES}


def build_program(phase_limit=99, dbg=False, interleave=True, parity=True):
    nc = bass.Bass("TRN2", target_bir_lowering=False,
                   detect_race_conditions=False)
    dti = lambda n, s, d: nc.dram_tensor(n, s, d, kind="ExternalInput").ap()

    x_d = dti("x", [L, C], F32)
    wqkvT_d = dti("wqkvT", [C, 3 * C], F32)
    bqkv_d = dti("bqkv", [1, 3 * C], F32)
    woutT_d = dti("woutT", [C, C], F32)
    bout_d = dti("bout", [1, C], F32)
    gq_d = dti("gq", [128, NP], F32)
    gk_d = dti("gk", [128, NP], F32)
    cos_d = dti("cosr", [L, 512], F32)
    sin_d = dti("sinr", [L, 512], F32)
    cw1_d = dti("cw1", [C, C], F32)
    cw2_d = dti("cw2", [C, 512], F32)
    cw3_d = dti("cw3", [512, 256], F32)
    cw4_d = dti("cw4", [256, 1], F32)
    cb1_d = dti("cb1", [128, 8], F32)
    cb2_d = dti("cb2", [128, 4], F32)
    cb3_d = dti("cb3", [128, 2], F32)
    cb4_d = dti("cb4", [1, 1], F32)
    gw1_d = dti("gw1", [C, 512], F32)
    gw2_d = dti("gw2", [512, 256], F32)
    gw3_d = dti("gw3", [256, 1], F32)
    gb1_d = dti("gb1", [128, 4], F32)
    gb2_d = dti("gb2", [128, 2], F32)
    gb3_d = dti("gb3", [1, 1], F32)
    ar_d = dti("ar", [1, 1], F32)

    y_d = nc.dram_tensor("y", [L, C], F32, kind="ExternalOutput").ap()
    ik = "ExternalOutput" if dbg else "Internal"
    if dbg:
        dbg_d = nc.dram_tensor("dbgpre", [L, C], F32, kind="ExternalOutput").ap()
        dbgo_d = nc.dram_tensor("dbgotn", [NST, 128, 4096], BF16, kind="ExternalOutput").ap()
        dbgcc_d = nc.dram_tensor("dbgcc", [24 * 128, 8], F32, kind="ExternalOutput").ap()
        dbgob_d = nc.dram_tensor("dbgob", [24 * 128, 4096], BF16, kind="ExternalOutput").ap()
        dbgh1_d = nc.dram_tensor("dbgh1", [24 * 128, 4096], BF16, kind="ExternalOutput").ap()
        dbgh2_d = nc.dram_tensor("dbgh2", [24 * 128, 2048], BF16, kind="ExternalOutput").ap()
        dbgh3_d = nc.dram_tensor("dbgh3", [24 * 128, 1024], BF16, kind="ExternalOutput").ap()
        dbgtf_d = nc.dram_tensor("dbgtf", [24, 512], F32, kind="ExternalOutput").ap()
        dbgcab_d = nc.dram_tensor("dbgcab", [128, 2], F32, kind="ExternalOutput").ap()
    qsp_d = nc.dram_tensor("qsp", [NT, 128, 128 * NP], BF16, kind=ik).ap()
    str_d = nc.dram_tensor("strm", [L, C], F32, kind=ik).ap()
    strb_d = nc.dram_tensor("strmb", [L, C], BF16, kind="Internal").ap()
    ssp_d = nc.dram_tensor("ssp", [L, C], F32, kind=ik).ap()

    es = contextlib.ExitStack()
    sbt = lambda n, s, d: es.enter_context(nc.sbuf_tensor(n, s, d))
    sem_names = ["sB", "sIN", "sWL", "sX", "sCS", "sTP", "sEVX", "sMM", "sVT",
                 "sRP", "sQS", "sQL", "sPE", "sEX", "sPV", "sEV", "sTR", "sDV",
                 "sXL", "sSP", "sOUT", "sSC", "sMN", "sGP", "sST", "sQP"]
    sems = {n: es.enter_context(nc.semaphore(n)) for n in sem_names}
    P = Plan(nc, sems)

    identb = sbt("identb", [128, 128], BF16)
    identf = sbt("identf", [128, 128], F32)
    onesr = sbt("onesr", [1, 128], BF16)
    onesc = sbt("onesc", [128, 1], BF16)
    onesrf = sbt("onesrf", [1, 128], F32)
    macc = sbt("macc", [128, 1024], F32)
    gpc = sbt("gpc", [128, NP], F32)
    BIAS = sbt("BIAS", [128, NP], F32)
    epsc = sbt("epsc", [128, 4], F32)
    qsmp = sbt("qsmp", [128, NP * 64], BF16)
    cabc = sbt("cabc", [128, 2], F32)
    gqc = sbt("gqc", [128, NP], F32)
    gkc = sbt("gkc", [128, NP], F32)
    barx = sbt("barx", [128, 2], F32)
    es2 = contextlib.ExitStack()
    vt = es2.enter_context(nc.sbuf_tensor("vt", [128, NT * H * 65], BF16))
    kT = es2.enter_context(nc.sbuf_tensor("kT", [128, NP * L], BF16))
    PSA = es.enter_context(nc.psum_tensor("PSA", [128, 4096], F32))
    PSAb = PSA[:].bitcast(BF16)
    P.bar_sb = barx
    P.bar_ps = PSAb[0:1, 4094:4095]
    P.bar_in = identb[0:1, 0:1]
    vt4 = vt[:].rearrange("p (t h c) -> p t h c", t=NT, h=H)


    def _stub(close_es2):
        if close_es2:
            es2.close()
        with contextlib.ExitStack() as ph2:
            yts = ph2.enter_context(nc.sbuf_tensor("ytstub", [128, C], F32))
            P.op("gpsimd", mk("memset", yts[:], 0.0))
            mz = P.after("gpsimd", "sGP")
            for t in range(NT):
                P.op("sync", mk("dma_start", out=y_d[128 * t:128 * t + 128, :],
                                in_=yts[:]),
                     wait=[("sGP", mz)] if t == 0 else [],
                     inc=(("sOUT", 16),))
            P.op("sync", mk("nop"), wait=[("sOUT", P.val("sOUT"))])
            P.flush()
        es.close()
        return nc

    P.sop("gpsimd", mk("memset", barx[:], 0.0))
    P.sop("gpsimd", mk("memset", identb[:], 0.0))
    P.sop("gpsimd", mk("affine_select", out=identb[:], in_=identb[:],
                       compare_op=ALU.not_equal, fill=1.0, base=0,
                       pattern=[[-1, 128]], channel_multiplier=1))
    P.bar_wait = [("sGP", P.val("sGP"))]
    P.sop("gpsimd", mk("memset", identf[:], 0.0))
    P.sop("gpsimd", mk("affine_select", out=identf[:], in_=identf[:],
                       compare_op=ALU.not_equal, fill=1.0, base=0,
                       pattern=[[-1, 128]], channel_multiplier=1))
    P.op("gpsimd", mk("memset", onesr[:], 1.0))
    P.op("gpsimd", mk("memset", onesc[:], 1.0))
    P.op("gpsimd", mk("memset", onesrf[:], 1.0))
    P.op("gpsimd", mk("memset", epsc[:, 0:1], RMS_EPS))
    P.op("gpsimd", mk("memset", epsc[:, 1:2], float(np.log(0.125))))
    P.op("gpsimd", mk("memset", epsc[:, 2:3], LN_EPS))
    P.op("gpsimd", mk("memset", vt4[:, :, :, 64:65], 1.0))
    P.op("sync", mk("dma_start", out=gqc[:], in_=gq_d[:]), inc=(("sIN", 16),))
    P.op("sync", mk("dma_start", out=gkc[:], in_=gk_d[:]), inc=(("sIN", 16),))
    P.op("vector", mk("tensor_mul", gpc[:], gqc[:], gkc[:]),
         wait=[("sIN", P.val("sIN"))])
    P.barrier()
    P.flush()
    if phase_limit <= 1:
        return _stub(True)

    def projection_pass(part):
        with contextlib.ExitStack() as ph:
            psb = lambda n, s, d: ph.enter_context(nc.sbuf_tensor(n, s, d))
            W = psb(f"W{part}", [128, 8 * C], BF16)
            brow = psb(f"br{part}", [1, C], BF16)
            xb = psb(f"xb{part}", [128, 2 * C], BF16)
            xT = psb(f"xT{part}", [128, 2 * C], BF16)
            drnp = psb(f"drn{part}", [128, 64], F32)
            col0 = {"q": 0, "k": C, "v": 2 * C}[part]
            for cc in range(8):
                P.op("gpsimd", mk("dma_start", out=W[:, 1024 * cc:1024 * cc + 1024],
                                  in_=wqkvT_d[128 * cc:128 * cc + 128, col0:col0 + C]),
                     inc=(("sWL", 16),))
            P.op("gpsimd", mk("dma_start", out=brow[:],
                              in_=bqkv_d[:, col0:col0 + C]), inc=(("sWL", 16),))
            w_ready = P.val("sWL")

            if part != "v":
                qf = psb(f"qf{part}", [128, 2 * C], F32)
                qb = psb(f"qb{part}", [128, 2 * C], BF16)
                cst = psb(f"cst{part}", [128, 2048], F32)
                rt = psb(f"rt{part}", [128, 1024], F32)
                rt2 = psb(f"rt2{part}", [128, 1024], F32)
                sqb = psb(f"sq{part}", [128, 1024], F32)
                sums = psb(f"sm{part}", [128, 96], F32)
                if part == "q":
                    qTst = psb("qTst", [128, 2 * C], BF16)

            pacc2 = (PSA[:, 0:1024], PSA[:, 1024:2048])
            ptX = PSAb[:, 4096:5120]
            ptQ = PSAb[:, 6144:7168]
            Z = lambda: [0] * (NT + 2)
            ev = {k: Z() for k in ("xl", "tp", "evx", "mm", "acc", "rp", "tp2",
                                   "qk", "cs", "spl", "qbd", "inv", "red",
                                   "gp3", "gp4", "csl")}

            def stageA(t):
                bi = t % 2
                w = []
                if t >= 2:
                    w.append(("sTP", ev["tp"][t - 2]))
                P.op("gpsimd", mk("dma_start", out=xb[:, C * bi:C * bi + C],
                                  in_=x_d[128 * t:128 * t + 128, :]),
                     wait=w, inc=(("sX", 16),))
                ev["xl"][t] = P.val("sX")
                if part != "v":
                    w = [("sCS", ev["cs"][t - 2])] if t >= 2 else []
                    P.op("sync", mk("dma_start", out=cst[:, 1024 * bi:1024 * bi + 512],
                                    in_=cos_d[128 * t:128 * t + 128, :]),
                         wait=w, inc=(("sIN", 16),))
                    P.op("sync", mk("dma_start",
                                    out=cst[:, 1024 * bi + 512:1024 * bi + 1024],
                                    in_=sin_d[128 * t:128 * t + 128, :]),
                         inc=(("sIN", 16),))
                    ev["csl"][t] = P.val("sIN")
                for cc in range(8):
                    w = []
                    if cc == 0:
                        w.append(("sX", ev["xl"][t]))
                        if t > 0:
                            w.append(("sEVX", ev["evx"][t - 1]))
                    P.op("tensor", mk("transpose", ptX[:, 128 * cc:128 * cc + 128],
                                      xb[:, C * bi + 128 * cc:C * bi + 128 * cc + 128],
                                      identb[:]), wait=w)
                ev["tp"][t] = P.after("tensor", "sTP")
                w = [("sTP", ev["tp"][t])]
                if t >= 2:
                    w.append(("sMM", ev["mm"][t - 2]))
                P.op("scalar", mk("copy", xT[:, C * bi:C * bi + C],
                                  ptX[:]), wait=w)
                P.op("scalar", mk("copy", drnp[:], identf[:, 0:64]))
                ev["evx"][t] = P.after("scalar", "sEVX")
                pacc = pacc2[bi]
                for cc in range(8):
                    for n in range(2):
                        w = []
                        if cc == 0 and n == 0:
                            w.append(("sEVX", ev["evx"][t]))
                            w.append(("sWL", w_ready))
                            if t >= 2:
                                w.append(("sVT", ev["acc"][t - 2]) if part == "v"
                                         else ("sRP", ev["rp"][t - 2]))
                        P.op("tensor", mk("matmul", pacc[:, 512 * n:512 * n + 512],
                                          xT[:, C * bi + 128 * cc:C * bi + 128 * cc + 128],
                                          W[:, 1024 * cc + 512 * n:1024 * cc + 512 * n + 512],
                                          start=(cc == 0), stop=False,
                                          skip_group_check=True), wait=w)
                for n in range(2):
                    P.op("tensor", mk("matmul", pacc[:, 512 * n:512 * n + 512],
                                      onesr[:], brow[:, 512 * n:512 * n + 512],
                                      start=False, stop=True,
                                      skip_group_check=True))
                ev["mm"][t] = P.after("tensor", "sMM")

            def stageB(t):
                bi = t % 2
                pacc = pacc2[bi]
                if part == "v":
                    P.op("vector", mk("tensor_copy", vt4[:, t, :, 0:64],
                                      pacc[:].rearrange("p (h z) -> p h z", h=H)),
                         wait=[("sMM", ev["mm"][t])])
                    ev["acc"][t] = P.after("vector", "sVT")
                    return
                cs_loaded = ev["csl"][t]
                w = [("sMM", ev["mm"][t])]
                if t >= 2:
                    w.append(("sDV", ev["qbd"][t - 2]))
                qfb = qf[:, C * bi:C * bi + C]
                P.op("scalar", mk("copy", qfb, pacc[:]), wait=w)
                ev["rp"][t] = P.after("scalar", "sRP")
                he = qfb.rearrange("p (h z) -> p h z", h=H)[:, :, 0:32]
                ho = qfb.rearrange("p (h z) -> p h z", h=H)[:, :, 32:64]
                ce = cst[:, 1024 * bi:1024 * bi + 512].rearrange("p (h z) -> p h z", h=H)
                se = cst[:, 1024 * bi + 512:1024 * bi + 1024].rearrange("p (h z) -> p h z", h=H)
                t1 = rt[:, 0:512].rearrange("p (h z) -> p h z", h=H)
                t2 = rt[:, 512:1024].rearrange("p (h z) -> p h z", h=H)
                t3 = rt2[:, 0:512].rearrange("p (h z) -> p h z", h=H)
                t4 = rt2[:, 512:1024].rearrange("p (h z) -> p h z", h=H)
                w = [("sIN", cs_loaded), ("sRP", ev["rp"][t])]
                if t >= 1:
                    w.append(("sCS", ev["cs"][t - 1]))
                P.op("gpsimd", mk("tensor_mul", t3, he, se), wait=w)
                ev["gp3"][t] = P.after("gpsimd", "sGP")
                P.op("gpsimd", mk("tensor_mul", t4, ho, ce))
                ev["gp4"][t] = P.after("gpsimd", "sGP")
                P.op("vector", mk("tensor_mul", t1, he, ce),
                     wait=[("sIN", cs_loaded), ("sRP", ev["rp"][t])])
                P.op("vector", mk("tensor_mul", t2, ho, se))
                P.op("vector", mk("tensor_mul", sqb[:], qfb, qfb))
                P.op("vector", mk("tensor_sub", he, t1, t2),
                     wait=[("sGP", ev["gp3"][t])])
                P.op("vector", mk("tensor_add", ho, t3, t4),
                     wait=[("sGP", ev["gp4"][t])])
                ev["cs"][t] = P.after("vector", "sCS")
                sums_done = P.sop("vector", mk("tensor_reduce",
                                               sums[:, 48 * bi:48 * bi + 16],
                                               sqb[:].rearrange("p (h z) -> p h z", h=H),
                                               AX.X, ALU.add))
                ev["red"][t] = sums_done
                P.sop("scalar", mk("activation", sums[:, 48 * bi + 16:48 * bi + 32],
                                   sums[:, 48 * bi:48 * bi + 16],
                                   AF.Ln, bias=epsc[:, 0:1], scale=1.0 / 64.0),
                      wait=[("sDV", sums_done)])
                if part == "q":
                    ev["inv"][t] = P.sop("scalar",
                                         mk("activation",
                                            sums[:, 48 * bi + 32:48 * bi + 48],
                                            sums[:, 48 * bi + 16:48 * bi + 32],
                                            AF.Exp, bias=epsc[:, 1:2],
                                            scale=-0.5))
                else:
                    ev["inv"][t] = P.sop("scalar",
                                         mk("activation",
                                            sums[:, 48 * bi + 32:48 * bi + 48],
                                            sums[:, 48 * bi + 16:48 * bi + 32],
                                            AF.Exp, scale=-0.5))

            def stageC(t):
                if part == "v":
                    return
                bi = t % 2
                qfb = qf[:, C * bi:C * bi + C]
                qbb = qb[:, C * bi:C * bi + C]
                for h in range(H):
                    w = []
                    if h == 0:
                        w.append(("sSC", ev["inv"][t]))
                        if t >= 2:
                            w.append(("sTP", ev["tp2"][t - 2]))
                    P.op("vector", mk("tensor_scalar", qbb[:, 64 * h:64 * h + 64],
                                      qfb[:, 64 * h:64 * h + 64],
                                      sums[:, 48 * bi + 32 + h:48 * bi + 33 + h],
                                      None, ALU.mult),
                         wait=w)
                P.op("vector", mk("tensor_copy", sqb[:, 0:128], sqb[:, 128:256]))
                ev["qbd"][t] = P.after("vector", "sDV")
                for cc in range(8):
                    w = []
                    if cc == 0:
                        w.append(("sDV", ev["qbd"][t]))
                        if t > 0:
                            w.append(("sQS", ev["qk"][t - 1]))
                    P.op("tensor", mk("transpose", ptQ[:, 128 * cc:128 * cc + 128],
                                      qbb[:, 128 * cc:128 * cc + 128], identb[:]),
                         wait=w)
                ev["tp2"][t] = P.after("tensor", "sTP")
                if part == "q":
                    w = [("sTP", ev["tp2"][t])]
                    if t >= 2:
                        w.append(("sQP", ev["spl"][t - 2]))
                    P.op("scalar", mk("copy", qTst[:, C * bi:C * bi + C],
                                      ptQ[:]), wait=w)
                    qtc = P.after("scalar", "sSC")
                    smp_dst = qsmp[:].rearrange("p (j s) -> p j s", j=NP)[:, :, 2 * t:2 * t + 2]
                    smp_src = qTst[:, C * bi:C * bi + C].rearrange("p (j s) -> p j s", j=NP)[:, :, 0:128:64]
                    P.op("vector", mk("tensor_copy", smp_dst, smp_src),
                         wait=[("sSC", qtc)])
                    ev["qk"][t] = P.after("vector", "sQS")
                    P.op("scalar", mk("dma_start", out=qsp_d[t],
                                      in_=qTst[:, C * bi:C * bi + C]),
                         wait=[("sQS", ev["qk"][t])], inc=(("sQP", 16),))
                    ev["spl"][t] = P.val("sQP")
                else:
                    for jj in range(NP):
                        w = [("sTP", ev["tp2"][t])] if jj == 0 else []
                        P.op("vector", mk("tensor_scalar",
                                          kT[:, L * jj + 128 * t:L * jj + 128 * t + 128],
                                          ptQ[:, 128 * jj:128 * jj + 128],
                                          gpc[:, jj:jj + 1], None, ALU.mult),
                             wait=w)
                    ev["qk"][t] = P.after("vector", "sQS")

            for i in range(NT + 2):
                if i < NT:
                    stageA(i)
                if 0 <= i - 1 < NT:
                    stageB(i - 1)
                if 0 <= i - 2 < NT:
                    stageC(i - 2)
            P.barrier()
            P.flush()


    projection_pass("v")
    if phase_limit <= 2:
        return _stub(True)
    projection_pass("q")
    if phase_limit <= 3:
        return _stub(True)
    projection_pass("k")
    if phase_limit <= 4:
        return _stub(True)

    with contextlib.ExitStack() as ph:
        scr = ph.enter_context(nc.sbuf_tensor("smx", [128, 4], F32))
        nmb = ph.enter_context(nc.sbuf_tensor("nmb", [1, NP], BF16))
        psmpA = PSA[:, 0:64]
        psmpB = PSA[:, 512:576]
        ptr1 = PSA[0:1, 1536:1664]
        pbc = PSA[:, 1024:1024 + NP]
        prev_dv = 0
        prev_tr = 0
        for j in range(NP):
            kj = kT[:, L * j:L * (j + 1)]
            w = [("sDV", prev_dv)] if j else []
            P.op("tensor", mk("matmul", psmpA,
                              kj[0:64, 0:L:32], qsmp[0:64, 64 * j:64 * j + 64],
                              start=True, stop=True, tile_position=(0, 0),
                              skip_group_check=True), wait=w)
            P.op("tensor", mk("matmul", psmpB,
                              kj[64:128, 0:L:32], qsmp[64:128, 64 * j:64 * j + 64],
                              start=True, stop=True, tile_position=(64, 0),
                              skip_group_check=True))
            mm_done = P.after("tensor", "sPE")
            w = [("sPE", mm_done)]
            if j:
                w.append(("sPE", prev_tr))
            P.sop("vector", mk("reduce_max", scr[:, 0:1], psmpA, AX.X), wait=w)
            P.sop("vector", mk("reduce_max", scr[:, 1:2], psmpB, AX.X))
            col_done = P.sop("vector", mk("tensor_max", scr[:, 0:1],
                                          scr[:, 0:1], scr[:, 1:2]))
            P.op("tensor", mk("transpose", ptr1[:], scr[:, 0:1], identf[:]),
                 wait=[("sDV", col_done)])
            prev_tr = P.after("tensor", "sPE")
            P.sop("vector", mk("reduce_max", scr[0:1, 1:2], ptr1[:], AX.X),
                  wait=[("sPE", prev_tr)])
            prev_dv = P.sop("vector", mk("tensor_scalar", nmb[:, j:j + 1],
                                         scr[0:1, 1:2], -1.0, None, ALU.mult))
        P.op("tensor", mk("matmul", pbc[:], onesr[:], nmb[:],
                          start=True, stop=True, skip_group_check=True),
             wait=[("sDV", prev_dv)])
        bmm = P.after("tensor", "sPE")
        P.op("vector", mk("tensor_copy", BIAS[:], pbc[:]), wait=[("sPE", bmm)])
        P.barrier()
        P.flush()
    if phase_limit <= 5:
        return _stub(True)

    with contextlib.ExitStack() as ph:
        psb = lambda n, s, d: ph.enter_context(nc.sbuf_tensor(n, s, d))
        qbuf = psb("qbuf", [128, 2 * SQT], BF16)
        PT = psb("PT", [128, 2 * 1024], BF16)
        oraw = psb("oraw", [128, 2 * 1024], BF16)
        ostg = psb("ostg", [128, 2 * 512], BF16)
        invz = psb("invz", [128, 4], F32)
        oTn = psb("oTn", [128, 2 * 4096], BF16)
        wout = psb("wout", [128, 8192], BF16)
        borow = psb("borow", [1, C], BF16)
        xr = psb("xr", [128, C], F32)
        strm = psb("strmt", [128, C], F32)
        lncols = psb("lnc", [128, 8], F32)
        sqt_ = psb("sqt", [128, C], F32)

        psTb = (PSA[:, 0:1024], PSA[:, 1024:2048])
        if parity:
            poAp = (PSA[:, 2048:2560], PSA[:, 3072:3584])
            poBp = (PSA[:, 2560:3072], PSA[:, 3584:4096])
            oQbase = (4096, 6144)
            pout = PSA[:, 3072:4096]
        else:
            poAp = (PSA[:, 2048:2560],) * 2
            poBp = (PSA[:, 2560:3072],) * 2
            oQbase = (4096, 4096)
            pout = PSA[:, 3072:4096]

        for cc in range(8):
            P.op("gpsimd", mk("dma_start", out=wout[:, 1024 * cc:1024 * cc + 1024],
                              in_=woutT_d[128 * cc:128 * cc + 128, :]),
                 inc=(("sWL", 16),))
        P.op("gpsimd", mk("dma_start", out=borow[:], in_=bout_d[:]),
             inc=(("sWL", 16),))
        wout_ready = P.val("sWL")

        qload = {}

        def load_qhat(st, j, wait):
            b = (st * NP + j) % 2
            src = qsp_d[4 * st:4 * st + 4, :, 128 * j:128 * j + 128].transpose([1, 0, 2])
            dst = qbuf[:, SQT * b:SQT * b + SQT].rearrange("p (t z) -> p t z", t=4)
            P.op("sync", mk("dma_start", out=dst, in_=src), wait=wait,
                 inc=(("sQL", 16),))
            qload[(st, j)] = P.val("sQL")

        load_qhat(0, 0, [])
        load_qhat(0, 1, [])

        exp_done = {}
        pv_done = {}
        epi_dv = {}
        epi_parts = {}
        last_out_dv = 0

        def plan_epilogue_A(st, j):
            par = j % 2
            poA, poB = poAp[par], poBp[par]
            orw = oraw[:, 1024 * par:1024 * par + 1024]
            P.op("vector", mk("tensor_copy", orw[0:65, 0:512], poA[0:65, :]),
                 wait=[("sPV", pv_done[(st, j, 31)])])
            P.op("vector", mk("tensor_copy", orw[0:65, 512:1024], poB[0:65, :]))
            ev_done = P.after("vector", "sEV")
            base = oQbase[par]
            for q in range(4):
                w = [("sEV", ev_done)] if q == 0 else []
                P.op("tensor", mk("transpose",
                                  PSAb[:, base + 256 * q:base + 256 * q + 65],
                                  orw[0:65, 128 * q:128 * q + 128],
                                  identb[0:65, 0:65]), wait=w)
                P.op("tensor", mk("transpose",
                                  PSAb[:, base + 256 * q + 128:base + 256 * q + 193],
                                  orw[0:65, 512 + 128 * q:512 + 128 * q + 128],
                                  identb[0:65, 0:65]))
            tr1 = P.after("tensor", "sTR")
            stg = {}
            for q in range(4):
                oQA = PSAb[:, base + 256 * q:base + 256 * q + 128]
                oQB = PSAb[:, base + 256 * q + 128:base + 256 * q + 256]
                w = [("sTR", tr1)] if q == 0 else []
                P.sop("vector", mk("reciprocal", invz[:, 2 * par:2 * par + 1],
                                   oQA[:, 64:65]), wait=w)
                P.sop("vector", mk("reciprocal", invz[:, 2 * par + 1:2 * par + 2],
                                   oQB[:, 64:65]))
                P.sop("vector", mk("tensor_scalar",
                                   ostg[:, 512 * par + 128 * q:512 * par + 128 * q + 64],
                                   oQA[:, 0:64], invz[:, 2 * par:2 * par + 1],
                                   None, ALU.mult))
                stg[q] = P.sop("vector", mk("tensor_scalar",
                                            ostg[:, 512 * par + 128 * q + 64:512 * par + 128 * q + 128],
                                            oQB[:, 0:64], invz[:, 2 * par + 1:2 * par + 2],
                                            None, ALU.mult))
            return stg

        def plan_epilogue_B(st, j, stg):
            par = j % 2
            base = oQbase[par]
            for q in range(4):
                P.op("tensor", mk("transpose",
                                  PSAb[:, base + 1024 + 128 * q:base + 1024 + 128 * q + 128],
                                  ostg[:, 512 * par + 128 * q:512 * par + 128 * q + 128],
                                  identb[:]), wait=[("sDV", stg[q])])
            tr2_all = P.after("tensor", "sTR")
            last = 0
            ob_ = oTn[:, 4096 * (st % 2):4096 * (st % 2) + 4096]
            for q in range(4):
                oNN = PSAb[:, base + 1024 + 128 * q:base + 1024 + 128 * q + 128]
                P.op("vector", mk("tensor_copy",
                                  ob_[:, 1024 * q + 128 * j:1024 * q + 128 * j + 128],
                                  oNN), wait=[("sTR", tr2_all)] if q == 0 else [])
                last = P.after("vector", "sDV")
            epi_dv[(st, j)] = last

        def plan_outproj_qtile(ost, q, state):
            ob_ = oTn[:, 4096 * (ost % 2):4096 * (ost % 2) + 4096]
            for kj in range(8):
                for n in range(2):
                    w = []
                    if kj == 0 and n == 0:
                        if q == 0:
                            w += state["all_epi"]()
                            w.append(("sWL", wout_ready))
                        else:
                            w.append(("sDV", state["add_dvs"][q - 1]))
                    P.op("tensor", mk("matmul", pout[:, 512 * n:512 * n + 512],
                                      ob_[:, 1024 * q + 128 * kj:1024 * q + 128 * kj + 128],
                                      wout[:, 1024 * kj + 512 * n:1024 * kj + 512 * n + 512],
                                      start=(kj == 0), stop=False,
                                      skip_group_check=True), wait=w)
            for n in range(2):
                P.op("tensor", mk("matmul", pout[:, 512 * n:512 * n + 512],
                                  onesr[:], borow[:, 512 * n:512 * n + 512],
                                  start=False, stop=True,
                                  skip_group_check=True))
            op_done = P.after("tensor", "sPE")
            trow = SQT * ost + 128 * q
            w = [("sDV", P.val("sDV"))] if (ost or q) else []
            P.op("sync", mk("dma_start", out=xr[:],
                            in_=x_d[trow:trow + 128, :]), wait=w,
                 inc=(("sXL", 16),))
            x_loaded = P.val("sXL")
            w = [("sPE", op_done), ("sXL", x_loaded),
                 ("sSP", P.val("sSP"))]
            P.op("vector", mk("tensor_add", strm[:], pout[:], xr[:]), wait=w)
            state["add_dvs"][q] = P.after("vector", "sDV")
            if dbg:
                P.op("gpsimd", mk("dma_start",
                                  out=dbg_d[trow:trow + 128, :],
                                  in_=strm[:]), wait=[("sDV", state["add_dvs"][q])],
                     inc=(("sSP", 16),))
                P.op("vector", mk("engine_nop"),
                     wait=[("sSP", P.val("sSP"))])
            P.sop("vector", mk("reduce_sum", lncols[:, 0:1], strm[:], AX.X))
            P.sop("vector", mk("scalar_tensor_tensor", sqt_[:], strm[:], 1.0,
                               strm[:], ALU.mult, ALU.mult,
                               accum_out=lncols[:, 1:2]))
            P.sop("vector", mk("tensor_scalar", lncols[:, 2:3], lncols[:, 0:1],
                               1.0 / C, None, ALU.mult))
            P.sop("vector", mk("tensor_mul", lncols[:, 3:4],
                               lncols[:, 2:3], lncols[:, 2:3]))
            P.sop("vector", mk("tensor_scalar", lncols[:, 4:5], lncols[:, 1:2],
                               1.0 / C, None, ALU.mult))
            var_done = P.sop("vector", mk("tensor_sub", lncols[:, 5:6],
                                          lncols[:, 4:5], lncols[:, 3:4]))
            P.sop("scalar", mk("activation", lncols[:, 6:7], lncols[:, 5:6],
                               AF.Ln, bias=epsc[:, 2:3]),
                  wait=[("sDV", var_done)])
            rstd_done = P.sop("scalar", mk("activation", lncols[:, 7:8],
                                           lncols[:, 6:7], AF.Exp,
                                           scale=-0.5))
            P.sop("vector", mk("tensor_scalar", strm[:], strm[:],
                               lncols[:, 2:3], lncols[:, 7:8],
                               ALU.subtract, ALU.mult),
                  wait=[("sSC", rstd_done)])
            ln_d = P.val("sDV")
            P.op("gpsimd", mk("dma_start", out=str_d[trow:trow + 128, :],
                              in_=strm[:]), wait=[("sDV", ln_d)],
                 inc=(("sSP", 16),))
            P.op("gpsimd", mk("dma_start", out=strb_d[trow:trow + 128, :],
                              in_=strm[:]), wait=[("sDV", ln_d)],
                 inc=(("sSP", 16),))
            if ost == 0 and q == 0:
                P.sop("vector", mk("tensor_copy", macc[:], strm[:]))
            else:
                P.sop("vector", mk("tensor_add", macc[:], macc[:], strm[:]))

        pend_state = {}
        for st in range(NST):
            for j in range(NP):
                b = (st * NP + j) % 2
                par = j % 2
                poA, poB = poAp[par], poBp[par]
                pj = None if (st, j) == (0, 0) else \
                    ((st, j - 1) if j > 0 else (st - 1, NP - 1))
                prev2 = None if pj is None or pj == (0, 0) and (st, j) == (0, 1) \
                    else None
                i_lin = st * NP + j
                back = 2 if parity else 1
                if i_lin >= back:
                    pp = ((i_lin - back) // NP, (i_lin - back) % NP)
                else:
                    pp = None

                def emit_pv(c):
                    first = True
                    for a in range(2):
                        po = poA if a == 0 else poB
                        h = 2 * j + a
                        vsl = vt[:, 1040 * c + 65 * h:1040 * c + 65 * h + 65]
                        w = []
                        if first:
                            w.append(("sEX", exp_done[(st, j, c)]))
                            if c == 0:
                                if pp is not None:
                                    w.append(("sDV", epi_dv[pp]))
                                if (par == 1 or not parity) and last_out_dv:
                                    w.append(("sDV", last_out_dv))
                        first = False
                        P.op("tensor", mk("matmul", po[0:65, :], vsl,
                                          PT[:, 1024 * (c % 2) + 512 * a:
                                              1024 * (c % 2) + 512 * a + 512],
                                          start=(c == 0), stop=(c == 31),
                                          skip_group_check=True), wait=w)
                    pv_done[(st, j, c)] = P.after("tensor", "sPV")

                stg_prev = None
                for c in range(32):
                    w = []
                    if c == 0:
                        w.append(("sQL", qload[(st, j)]))
                    if c >= 2:
                        w.append(("sEX", exp_done[(st, j, c - 2)]))
                    elif pj is not None:
                        w.append(("sEX", exp_done[(pj[0], pj[1], 30 + c)]))
                    pb = psTb[c % 2]
                    P.op("tensor", mk("matmul", pb[:, 0:512],
                                      kT[0:64, L * j + 128 * c:L * j + 128 * c + 128],
                                      qbuf[0:64, SQT * b:SQT * b + 512],
                                      start=True, stop=True,
                                      tile_position=(0, 0),
                                      skip_group_check=True), wait=w)
                    P.op("tensor", mk("matmul", pb[:, 512:1024],
                                      kT[64:128, L * j + 128 * c:L * j + 128 * c + 128],
                                      qbuf[64:128, SQT * b:SQT * b + 512],
                                      start=True, stop=True,
                                      tile_position=(64, 0),
                                      skip_group_check=True))
                    st_done = P.after("tensor", "sPE")
                    w = [("sPE", st_done)]
                    if c >= 2:
                        w.append(("sPV", pv_done[(st, j, c - 2)]))
                    elif pj is not None:
                        w.append(("sPV", pv_done[(pj[0], pj[1], 30 + c)]))
                    P.op("scalar", mk("activation",
                                      PT[:, 1024 * (c % 2):1024 * (c % 2) + 1024],
                                      pb[:], AF.Exp, bias=BIAS[:, j:j + 1]),
                         wait=w)
                    exp_done[(st, j, c)] = P.after("scalar", "sEX")
                    if c >= 1:
                        emit_pv(c - 1)
                    if interleave and c == 2 and pj is not None and pj not in epi_dv:
                        stg_prev = plan_epilogue_A(pj[0], pj[1])
                    if interleave and c == 8 and pj is not None and stg_prev is not None and pj not in epi_dv:
                        plan_epilogue_B(pj[0], pj[1], stg_prev)
                        stg_prev = None
                    if j == 0 and st >= 1 and c in (10, 15, 20, 25):
                        q_o = (10, 15, 20, 25).index(c)
                        plan_outproj_qtile(st - 1, q_o, pend_state[st - 1])
                        if q_o == 3:
                            last_out_dv = pend_state[st - 1]["add_dvs"][3]
                    if c == 31:
                        nxt = (st + (j + 2) // NP, (j + 2) % NP)
                        if nxt[0] < NST:
                            load_qhat(nxt[0], nxt[1],
                                      [("sEX", exp_done[(st, j, 31)])])
                emit_pv(31)
                if (st, j) not in epi_dv and (not interleave or
                                               (j == NP - 1 and st == NST - 1)):
                    stg_last = plan_epilogue_A(st, j)
                    plan_epilogue_B(st, j, stg_last)

            pend_state[st] = {"all_epi": (lambda st=st:
                                          [("sDV", epi_dv[(st, jj)])
                                           for jj in range(NP)]),
                              "add_dvs": {}}
            if st == NST - 1:
                for q in range(4):
                    plan_outproj_qtile(st, q, pend_state[st])
                last_out_dv = pend_state[st]["add_dvs"][3]

        P.barrier()
        P.flush()
    es2.close()

    with contextlib.ExitStack() as ph:
        psb = lambda n, s, d: ph.enter_context(nc.sbuf_tensor(n, s, d))
        cw1 = psb("cw1s", [128, 8192], BF16)
        cw2 = psb("cw2s", [128, 4096], BF16)
        cw3 = psb("cw3s", [128, 1024], BF16)
        cw4 = psb("cw4s", [128, 2], BF16)
        cb1s = psb("cb1s", [128, 8], F32)
        cb2s = psb("cb2s", [128, 4], F32)
        cb3s = psb("cb3s", [128, 2], F32)
        cb4s = psb("cb4s", [1, 1], F32)
        gw1 = psb("gw1s", [128, 4096], BF16)
        gw2 = psb("gw2s", [128, 1024], BF16)
        gw3 = psb("gw3s", [128, 2], BF16)
        gb1s = psb("gb1s", [128, 4], F32)
        gb2s = psb("gb2s", [128, 2], F32)
        gb3s = psb("gb3s", [1, 1], F32)
        ars = psb("ars", [1, 1], F32)
        gmisc = psb("gmisc", [128, 16], F32)
        gcrow = psb("gcrow", [1, C], BF16)
        gcT = psb("gcT", [128, 8], BF16)
        gact = psb("gact", [128, 12], F32)
        gbf = psb("gbf", [128, 8], BF16)
        cabrow = psb("cabrow", [1, 2], F32)

        x1t = psb("x1t", [128, 2 * C], F32)
        onesmf = psb("onesmf", [128, 1], F32)
        outt = psb("outt", [128, 4 * C], F32)
        ob = psb("ob", [128, 4 * C], BF16)
        obT = psb("obT", [128, 4096], BF16)
        h1b = psb("h1b", [128, 4096], BF16)
        h2b = psb("h2b", [128, 2048], BF16)
        h3b = psb("h3b", [128, 1024], BF16)
        estg = psb("estg", [128, 512], F32)
        xh = psb("xh", [128, 512], F32)
        tfrow = psb("tfrow", [1, 512], F32)
        tfrow2 = psb("tfrow2", [1, 512], F32)
        tfT = psb("tfT", [128, 4], F32)
        drn = psb("drn", [128, 256], F32)
        obc = psb("obc", [128, 2 * 4096], BF16)
        ccols = psb("ccols", [128, 8], F32)
        twt = psb("twt", [128, 2 * C], F32)

        def wload(dst, src_d, kdim, mdim):
            for k in range(kdim // 128):
                P.op("gpsimd", mk("dma_start",
                                  out=dst[:, k * mdim:(k + 1) * mdim],
                                  in_=src_d[128 * k:128 * k + 128, :]),
                     inc=(("sWL", 16),))

        wload(cw1, cw1_d, C, C)
        wload(cw2, cw2_d, C, 512)
        wload(cw3, cw3_d, 512, 256)
        P.op("gpsimd", mk("dma_start", out=cw4[:, 0:1], in_=cw4_d[0:128, :]),
             inc=(("sWL", 16),))
        P.op("gpsimd", mk("dma_start", out=cw4[:, 1:2], in_=cw4_d[128:256, :]),
             inc=(("sWL", 16),))
        wload(gw1, gw1_d, C, 512)
        wload(gw2, gw2_d, 512, 256)
        P.op("gpsimd", mk("dma_start", out=gw3[:, 0:1], in_=gw3_d[0:128, :]),
             inc=(("sWL", 16),))
        P.op("gpsimd", mk("dma_start", out=gw3[:, 1:2], in_=gw3_d[128:256, :]),
             inc=(("sWL", 16),))
        for dst, src in ((cb1s, cb1_d), (cb2s, cb2_d), (cb3s, cb3_d),
                         (cb4s, cb4_d), (gb1s, gb1_d), (gb2s, gb2_d),
                         (gb3s, gb3_d), (ars, ar_d)):
            P.op("sync", mk("dma_start", out=dst[:], in_=src[:]),
                 inc=(("sIN", 16),))
        w_ready = P.val("sWL")
        b_ready = P.val("sIN")
        obc_ld_next = {}
        for tt in range(4):
            P.op("sync", mk("dma_start", out=obc[:, C * tt:C * tt + C],
                            in_=strb_d[128 * tt:128 * tt + 128, :]),
                 inc=(("sST", 16),))
            obc_ld_next[tt] = P.val("sST")

        pmean = PSA[:, 0:1024]
        ones_mf = P.sop("vector", mk("reduce_sum", onesmf[:], identf[:], AX.X))
        for n in range(2):
            P.op("tensor", mk("matmul", pmean[0:1, 512 * n:512 * n + 512],
                              onesmf[:], macc[:, 512 * n:512 * n + 512],
                              start=True, stop=True,
                              skip_group_check=True),
                 wait=[("sDV", ones_mf)] if n == 0 else [])
        mean_done = P.after("tensor", "sDV")
        P.op("scalar", mk("activation", gcrow[:], pmean[0:1, :], AF.Copy,
                          scale=1.0 / L), wait=[("sDV", mean_done)])
        gc_done = P.after("scalar", "sSC")
        pgt = PSAb[:, 2048:2080]
        for cc in range(8):
            P.op("tensor", mk("transpose", pgt[:, 2 * cc:2 * cc + 1],
                              gcrow[0:1, 128 * cc:128 * cc + 128],
                              identb[0:1, 0:1]),
                 wait=[("sSC", gc_done)] if cc == 0 else [])
        gct_done = P.after("tensor", "sPE")
        gct_sb = P.sop("vector", mk("tensor_copy", gcT[:], pgt[:, 0:16:2]),
                       wait=[("sPE", gct_done)])

        def gchain(win, nk, nm, rhs_ap, pdst_fn, wait):
            for m in range(nm):
                for k in range(nk):
                    w = list(wait) if (m == 0 and k == 0) else []
                    P.op("tensor", mk("matmul", pdst_fn(m),
                                      win[:, k * (nm * 128) + m * 128:
                                          k * (nm * 128) + m * 128 + 128],
                                      rhs_ap(k),
                                      start=(k == 0), stop=(k == nk - 1),
                                      skip_group_check=True), wait=w)
            return P.after("tensor", "sPE")

        pg1 = lambda m: PSA[:, 512 * m + 384:512 * m + 385]
        pg2 = lambda m: PSA[:, 2048 + 512 * m + 384:2048 + 512 * m + 385]
        g1_done = gchain(gw1, 8, 4, lambda k: gcT[:, k:k + 1], pg1,
                         [("sDV", gct_sb), ("sWL", w_ready)])
        for m in range(4):
            w = [("sPE", g1_done), ("sIN", b_ready)] if m == 0 else []
            P.op("scalar", mk("activation", gbf[:, m:m + 1], pg1(m),
                              AF.Gelu, bias=gb1s[:, m:m + 1]), wait=w)
        g1b = P.after("scalar", "sSC")
        g2_done = gchain(gw2, 4, 2, lambda k: gbf[:, k:k + 1], pg2,
                         [("sSC", g1b)])
        for m in range(2):
            w = [("sPE", g2_done)] if m == 0 else []
            P.op("scalar", mk("activation", gbf[:, 4 + m:5 + m],
                              pg2(m), AF.Gelu,
                              bias=gb2s[:, m:m + 1]), wait=w)
        g2b = P.after("scalar", "sSC")
        pgf = PSA[0:1, 3072 + 384:3072 + 385]
        for k in range(2):
            P.op("tensor", mk("matmul", pgf[:], gw3[:, k:k + 1],
                              gbf[:, 4 + k:5 + k], start=(k == 0), stop=(k == 1),
                              skip_group_check=True),
                 wait=[("sSC", g2b)] if k == 0 else [])
        gf_mm = P.after("tensor", "sPE")
        gfb = P.sop("vector", mk("tensor_add", gmisc[0:1, 0:1], pgf[:],
                                 gb3s[:]), wait=[("sPE", gf_mm)])
        th_done = P.sop("scalar", mk("activation", gmisc[0:1, 5:6],
                                     gmisc[0:1, 0:1], AF.Tanh, scale=0.5),
                        wait=[("sDV", gfb)])
        P.sop("vector", mk("tensor_scalar", gmisc[0:1, 1:2], gmisc[0:1, 5:6],
                           0.5, 0.5, ALU.mult, ALU.add),
              wait=[("sSC", th_done)])
        P.sop("vector", mk("tensor_mul", gmisc[0:1, 2:3], gmisc[0:1, 1:2],
                           ars[:]))
        P.sop("vector", mk("tensor_scalar", gmisc[0:1, 3:4], ars[:], -1.0, 0.1,
                           ALU.mult, ALU.add))
        P.sop("vector", mk("scalar_tensor_tensor", cabrow[:, 0:1],
                           gmisc[0:1, 2:3], 1.4, gmisc[0:1, 3:4],
                           ALU.mult, ALU.add))
        cabh = P.sop("vector", mk("tensor_scalar", cabrow[:, 1:2], ars[:],
                                  0.3, None, ALU.mult))
        cab_done = P.sop("vector", mk("tensor_add", cabrow[:, 0:1],
                                      cabrow[:, 0:1], cabrow[:, 1:2]))
        cb4h = gmisc[0:1, 6:7]
        cab_done = P.sop("vector", mk("tensor_scalar", cb4h, cb4s[:], 0.5,
                                      None, ALU.mult))
        pcab = PSA[:, 3584 + 128:3584 + 130]
        P.op("tensor", mk("matmul", pcab[:], onesrf[:], cabrow[:],
                          start=True, stop=True, skip_group_check=True),
             wait=[("sDV", cab_done)])
        cab_mm = P.after("tensor", "sPE")
        cabd = P.sop("vector", mk("tensor_copy", cabc[:], pcab[:]),
              wait=[("sPE", cab_mm)])
        if dbg:
            P.op("gpsimd", mk("dma_start", out=dbgcab_d[:], in_=cabc[:]),
                 wait=[("sDV", cabd)], inc=(("sSP", 16),))
        P.barrier()

        ph1 = (PSA[:, 0:512], PSA[:, 512:1024])
        ptile3s = (PSAb[:, 2048:3072], PSAb[:, 6144:7168])
        ptf = PSA[0:1, 1536:2048]
        _ptfT_cols = (2304, 2816, 3584, 1792)
        ptfT = lambda cc: PSA[:, _ptfT_cols[cc]:_ptfT_cols[cc] + 1]
        spill_prev = {tt: P.val("sSP") for tt in range(4)}
        add_hist = {}
        obc_tp_prev = {tt: 0 for tt in range(4)}
        for s4 in range(8):
            obc_ld = dict(obc_ld_next)
            slot = (s4 % 2) * 4096
            if s4 < 7:
                nslot = ((s4 + 1) % 2) * 4096
                for tt in range(4):
                    t = 4 * (s4 + 1) + tt
                    w = []
                    if obc_tp_prev[tt]:
                        w.append(("sTP", obc_tp_prev[tt]))
                    P.op("sync", mk("dma_start",
                                    out=obc[:, nslot + C * tt:nslot + C * tt + C],
                                    in_=strb_d[128 * t:128 * t + 128, :]),
                         wait=w, inc=(("sST", 16),))
                    obc_ld_next[tt] = P.val("sST")
            ld_done = {}
            for tt in range(4):
                t = 4 * s4 + tt
                w = [("sSP", spill_prev[tt])]
                P.op("sync", mk("dma_start", out=outt[:, C * tt:C * tt + C],
                                in_=str_d[128 * t:128 * t + 128, :]), wait=w,
                     inc=(("sST", 16),))
                ld_done[tt] = P.val("sST")
            tw_done = {tt: 0 for tt in range(4)}
            copy_rdy = {}
            tp_hist = {}
            obT_hist = {}
            step = 0
            for cyc in range(3):
                obT_done = 0
                for tt in range(4):
                    bi = step % 2
                    pt3 = ptile3s[bi]
                    if cyc == 0:
                        src, scol = obc, slot + C * tt
                        w0 = [("sST", obc_ld[tt])]
                    else:
                        src, scol = ob, C * tt
                        w0 = [("sSC", copy_rdy[tt])]
                    for cc in range(8):
                        w = []
                        if cc == 0:
                            w = list(w0)
                            if step >= 2:
                                w.append(("sDV", obT_hist[step - 2]))
                        P.op("tensor", mk("transpose",
                                          pt3[:, 128 * cc:128 * cc + 128],
                                          src[:, scol + 128 * cc:scol + 128 * cc + 128],
                                          identb[:]), wait=w)
                    tp_hist[step] = P.after("tensor", "sTP")
                    if cyc == 0:
                        obc_tp_prev[tt] = tp_hist[step]
                    dst = obT[:].rearrange("p (c z) -> p c z", c=8)[:, :, 128 * tt:128 * tt + 128]
                    src = pt3[:].rearrange("p (c z) -> p c z", c=8)
                    P.op("vector", mk("tensor_copy", dst, src),
                         wait=[("sTP", tp_hist[step])])
                    obT_done = P.after("vector", "sDV")
                    obT_hist[step] = obT_done
                    step += 1

                def layer(win, bias, nk, nm, rhs, dst_b, wait_first):
                    gel = {}
                    for m in range(nm):
                        pb = ph1[m % 2]
                        for k in range(nk):
                            w = []
                            if k == 0:
                                if m == 0:
                                    w += wait_first
                                if m >= 2:
                                    w.append(("sSC", gel[m - 2]))
                            P.op("tensor", mk("matmul", pb[:],
                                              win[:, k * (nm * 128) + m * 128:
                                                  k * (nm * 128) + m * 128 + 128],
                                              rhs[:, 512 * k:512 * k + 512],
                                              start=(k == 0), stop=(k == nk - 1),
                                              skip_group_check=True), wait=w)
                        mmd = P.after("tensor", "sMM")
                        P.op("scalar", mk("activation",
                                          dst_b[:, 512 * m:512 * m + 512],
                                          pb[:], AF.Gelu,
                                          bias=bias[:, m:m + 1]),
                             wait=[("sMM", mmd)])
                        gel[m] = P.after("scalar", "sSC")
                    return gel[nm - 1]

                d1 = layer(cw1, cb1s, 8, 8, obT, h1b,
                           [("sDV", obT_done), ("sWL", w_ready),
                            ("sIN", b_ready)])
                if dbg:
                    r0 = (s4 * 3 + cyc) * 128
                    P.op("gpsimd", mk("dma_start",
                                      out=dbgob_d[r0:r0 + 128, :],
                                      in_=obT[:]),
                         wait=[("sDV", obT_done)], inc=(("sSP", 16),))
                    P.op("gpsimd", mk("dma_start",
                                      out=dbgh1_d[r0:r0 + 128, :],
                                      in_=h1b[:]),
                         wait=[("sSC", d1)], inc=(("sSP", 16),))
                d2 = layer(cw2, cb2s, 8, 4, h1b, h2b, [("sSC", d1)])
                if dbg:
                    P.op("gpsimd", mk("dma_start",
                                      out=dbgh2_d[r0:r0 + 128, :],
                                      in_=h2b[:, 0:2048]),
                         wait=[("sSC", d2)], inc=(("sSP", 16),))
                d3 = layer(cw3, cb3s, 4, 2, h2b, h3b, [("sSC", d2)])
                if dbg:
                    P.op("gpsimd", mk("dma_start",
                                      out=dbgh3_d[r0:r0 + 128, :],
                                      in_=h3b[:, 0:1024]),
                         wait=[("sSC", d3)], inc=(("sSP", 16),))
                for k in range(2):
                    w = []
                    if k == 0:
                        w = [("sSC", d3)]
                    P.op("tensor", mk("matmul", ptf[:], cw4[:, k:k + 1],
                                      h3b[:, 512 * k:512 * k + 512],
                                      start=(k == 0), stop=(k == 1),
                                      skip_group_check=True), wait=w)
                h4_done = P.after("tensor", "sMM")
                P.op("scalar", mk("activation", tfrow[:], ptf[:], AF.Tanh,
                                  bias=cb4h, scale=0.5), wait=[("sMM", h4_done)])
                tf_done = P.after("scalar", "sSC")
                P.op("vector", mk("tensor_copy", tfrow2[:], tfrow[:]),
                     wait=[("sSC", tf_done)])
                P.op("vector", mk("tensor_copy", drn[:, 0:128], identb[:]))
                tf2_done = P.after("vector", "sDV")
                for cc in range(4):
                    P.op("tensor", mk("transpose", ptfT(cc),
                                      tfrow2[0:1, 128 * cc:128 * cc + 128],
                                      identf[0:1, 0:1]),
                         wait=[("sDV", tf2_done)] if cc == 0 else [])
                tfT_mm = P.after("tensor", "sMM")
                for cc in range(4):
                    P.sop("vector", mk("tensor_scalar", ccols[:, cc:cc + 1],
                                       ptfT(cc), cabc[:, 1:2], cabc[:, 0:1],
                                       ALU.mult, ALU.add),
                          wait=[("sMM", tfT_mm)] if cc == 0 else [])
                cc_done = P.sop("vector", mk("tensor_scalar", ccols[:, 4:8],
                                             ccols[:, 0:4], -1.0, None,
                                             ALU.mult))
                if dbg:
                    r0 = (s4 * 3 + cyc) * 128
                    P.op("gpsimd", mk("dma_start",
                                      out=dbgcc_d[r0:r0 + 128, :],
                                      in_=ccols[:]),
                         wait=[("sDV", cc_done)], inc=(("sSP", 16),))
                    P.op("gpsimd", mk("dma_start",
                                      out=dbgtf_d[s4 * 3 + cyc:s4 * 3 + cyc + 1, :],
                                      in_=tfrow2[:]), inc=(("sSP", 16),))
                def prods(tt, w):
                    o = outt[:, C * tt:C * tt + C]
                    tw = twt[:, 1024 * (tt % 2):1024 * (tt % 2) + 1024]
                    cp = ccols[:, tt:tt + 1]
                    cn = ccols[:, 4 + tt:5 + tt]
                    P.sop("scalar", mk("mul", tw[:, 0:256], o[:, 512:768],
                                       cp), wait=w)
                    P.sop("scalar", mk("mul", tw[:, 256:512], o[:, 768:1024],
                                       cn))
                    P.sop("scalar", mk("mul", tw[:, 512:768], o[:, 0:256],
                                       cn))
                    return P.sop("scalar", mk("mul", tw[:, 768:1024],
                                              o[:, 256:512], cp))

                def dvadd(tt, pr):
                    o = outt[:, C * tt:C * tt + C]
                    tw = twt[:, 1024 * (tt % 2):1024 * (tt % 2) + 1024]
                    tw_done[tt] = ("sDV",
                                   P.sop("vector", mk("tensor_add", o, o,
                                                      tw[:]),
                                         wait=[("sSC", pr)]))

                def obcopy(tt):
                    P.sop("scalar", mk("copy", ob[:, C * tt:C * tt + C],
                                       outt[:, C * tt:C * tt + C]),
                          wait=[tw_done[tt]])
                    return P.sop("scalar", mk("copy", drn[:, 0:128],
                                              identf[:, 0:128]))

                w0 = [("sDV", cc_done)]
                if cyc == 0:
                    w0 += [("sST", ld_done[tt2]) for tt2 in range(4)]
                pr = {0: prods(0, w0)}
                pr[1] = prods(1, [])
                for tt in range(4):
                    dvadd(tt, pr[tt])
                    if cyc < 2:
                        copy_rdy[tt] = obcopy(tt)
                    if tt + 2 < 4:
                        w = [] if cyc < 2 else [tw_done[tt]]
                        pr[tt + 2] = prods(tt + 2, w)
                P.op("tensor", mk("ldweights", identb[:]),
                     wait=[tw_done[0]])

            for tt in range(4):
                t = 4 * s4 + tt
                bi = tt % 2
                w = [("sGP", add_hist[tt - 2])] if tt >= 2 else []
                P.op("sync", mk("dma_start", out=x1t[:, C * bi:C * bi + C],
                                in_=str_d[128 * t:128 * t + 128, :]),
                     wait=w, inc=(("sST", 16),))
                x1_ld = P.val("sST")
                P.op("gpsimd", mk("tensor_add", outt[:, C * tt:C * tt + C],
                                  outt[:, C * tt:C * tt + C],
                                  x1t[:, C * bi:C * bi + C]),
                     wait=[("sST", x1_ld), tw_done[tt]])
                add_hist[tt] = P.after("gpsimd", "sGP")
                P.op("gpsimd", mk("dma_start",
                                  out=ssp_d[128 * t:128 * t + 128, :],
                                  in_=outt[:, C * tt:C * tt + C]),
                     wait=[("sGP", add_hist[tt])], inc=(("sSP", 16),))
                spill_prev[tt] = P.val("sSP")
        P.barrier()
        P.flush()
    if phase_limit <= 7:
        return _stub(False)

    with contextlib.ExitStack() as ph:
        psb = lambda n, s, d: ph.enter_context(nc.sbuf_tensor(n, s, d))
        sin_ = psb("sin", [128, 4 * C], F32)
        sq2 = psb("sq2", [128, C], F32)
        yt = psb("yt", [128, C], F32)
        lnc = psb("lnc2", [128, 8], F32)
        norm_done = {}
        ld_val = {}
        def p3b_load(tp, w):
            qeng, qsem = (("sync", "sST") if tp % 2 == 0 else
                          ("scalar", "sQP"))
            P.op(qeng, mk("dma_start", out=sin_[:, C * (tp % 4):C * (tp % 4) + C],
                          in_=ssp_d[128 * tp:128 * tp + 128, :]),
                 wait=w, inc=((qsem, 16),))
            ld_val[tp] = (qsem, P.val(qsem))

        for tp in range(3):
            p3b_load(tp, [])
        for t in range(NT):
            bfr = sin_[:, C * (t % 4):C * (t % 4) + C]
            if t + 3 < NT:
                p3b_load(t + 3,
                         [("sDV", norm_done[t - 1])] if t >= 1 else [])
            P.sop("scalar", mk("activation", sq2[:], bfr, AF.Copy,
                               accum_out=lnc[:, 0:1]),
                  wait=[ld_val[t]])
            st2 = P.sop("scalar", mk("activation", sq2[:], bfr, AF.Square,
                                     accum_out=lnc[:, 1:2]))
            P.sop("vector", mk("tensor_scalar", lnc[:, 2:3], lnc[:, 0:1],
                               1.0 / C, None, ALU.mult),
                  wait=[("sSC", st2)])
            P.sop("vector", mk("tensor_mul", lnc[:, 3:4], lnc[:, 2:3],
                               lnc[:, 2:3]))
            P.sop("vector", mk("tensor_scalar", lnc[:, 4:5], lnc[:, 1:2],
                               1.0 / C, None, ALU.mult))
            var2 = P.sop("vector", mk("tensor_sub", lnc[:, 5:6], lnc[:, 4:5],
                                      lnc[:, 3:4]))
            P.sop("scalar", mk("activation", lnc[:, 6:7], lnc[:, 5:6], AF.Ln,
                               bias=epsc[:, 2:3]), wait=[("sDV", var2)])
            rstd2 = P.sop("scalar", mk("activation", lnc[:, 7:8], lnc[:, 6:7],
                                       AF.Exp, scale=-0.5))
            w = [("sSC", rstd2)]
            if t >= 1:
                w.append(("sOUT", 16 * t))
            norm_done_t = P.sop("vector", mk("tensor_scalar", yt[:], bfr,
                                             lnc[:, 2:3], lnc[:, 7:8],
                                             ALU.subtract, ALU.mult), wait=w)
            norm_done[t] = norm_done_t
            P.op("sync", mk("dma_start", out=y_d[128 * t:128 * t + 128, :],
                            in_=yt[:]), wait=[("sDV", norm_done[t])],
                 inc=(("sOUT", 16),))
        P.op("sync", mk("nop"), wait=[("sOUT", 16 * NT)])
        P.flush()

    es.close()
    return nc


def _host_prep(inputs):
    f32 = np.float32
    perm = np.concatenate([np.arange(0, D, 2), np.arange(1, D, 2)])
    Wqkv = np.asarray(inputs["L0_attn_w_qkv"], f32)
    bqkv = np.asarray(inputs["L0_attn_b_qkv"], f32)
    WT = np.ascontiguousarray(Wqkv.T)
    Wp = WT.copy()
    bp = bqkv.copy()
    for h in range(H):
        for base in (0, C):
            cols = base + h * D + perm
            Wp[:, base + h * D:base + (h + 1) * D] = WT[:, cols]
            bp[base + h * D:base + (h + 1) * D] = bqkv[cols]
    gq = np.asarray(inputs["L0_attn_q_gamma"], f32)[:, perm]
    gk = np.asarray(inputs["L0_attn_k_gamma"], f32)[:, perm]
    gq_cols = np.empty((128, NP), f32)
    gk_cols = np.empty((128, NP), f32)
    for j in range(NP):
        gq_cols[0:64, j] = gq[2 * j]
        gq_cols[64:128, j] = gq[2 * j + 1]
        gk_cols[0:64, j] = gk[2 * j]
        gk_cols[64:128, j] = gk[2 * j + 1]
    half = D // 2
    inv_freq = (1.0 / (10000.0 ** (np.arange(half, dtype=f32) / f32(half)))).astype(f32)
    ang = (np.arange(L, dtype=f32)[:, None] * inv_freq[None, :]).astype(f32)
    cosr = np.tile(np.cos(ang).astype(f32), (1, H))
    sinr = np.tile(np.sin(ang).astype(f32), (1, H))

    shared = dict(
        wqkvT=np.ascontiguousarray(Wp),
        bqkv=np.ascontiguousarray(bp[None, :]),
        woutT=np.ascontiguousarray(np.asarray(inputs["L0_attn_w_out"], f32).T),
        bout=np.ascontiguousarray(np.asarray(inputs["L0_attn_b_out"], f32)[None, :]),
        gq=gq_cols, gk=gk_cols, cosr=cosr, sinr=sinr,
        cw1=np.ascontiguousarray(np.asarray(inputs["L0_ffn_cn_w1"], f32)),
        cw2=np.ascontiguousarray(np.asarray(inputs["L0_ffn_cn_w2"], f32)),
        cw3=np.ascontiguousarray(np.asarray(inputs["L0_ffn_cn_w3"], f32)),
        cw4=np.ascontiguousarray(np.asarray(inputs["L0_ffn_cn_w4"], f32)),
        cb1=np.ascontiguousarray(np.asarray(inputs["L0_ffn_cn_b1"], f32).reshape(8, 128).T),
        cb2=np.ascontiguousarray(np.asarray(inputs["L0_ffn_cn_b2"], f32).reshape(4, 128).T),
        cb3=np.ascontiguousarray(np.asarray(inputs["L0_ffn_cn_b3"], f32).reshape(2, 128).T),
        cb4=np.asarray(inputs["L0_ffn_cn_b4"], f32).reshape(1, 1),
        gw1=np.ascontiguousarray(np.asarray(inputs["L0_ffn_gc_w1"], f32)),
        gw2=np.ascontiguousarray(np.asarray(inputs["L0_ffn_gc_w2"], f32)),
        gw3=np.ascontiguousarray(np.asarray(inputs["L0_ffn_gc_w3"], f32)),
        gb1=np.ascontiguousarray(np.asarray(inputs["L0_ffn_gc_b1"], f32).reshape(4, 128).T),
        gb2=np.ascontiguousarray(np.asarray(inputs["L0_ffn_gc_b2"], f32).reshape(2, 128).T),
        gb3=np.asarray(inputs["L0_ffn_gc_b3"], f32).reshape(1, 1),
        ar=np.asarray(inputs["L0_ffn_adaptive_range"], f32).reshape(1, 1),
    )
    x = np.asarray(inputs["L0_attn_x"], f32)
    return shared, x


def kernel(**inputs):
    shared, x = _host_prep(inputs)
    nc = build_program()
    in_maps = []
    for b in range(8):
        m = dict(shared)
        m["x"] = np.ascontiguousarray(x[b])
        in_maps.append(m)
    res = run_bass_kernel_spmd(nc, in_maps, core_ids=list(range(8)))
    out = np.stack([np.asarray(res.results[b]["y"]) for b in range(8)], axis=0)
    return out.astype(np.float32)


# version 22
# speedup vs baseline: 1.2066x; 1.2066x over previous
import contextlib
import numpy as np

import concourse.bass as bass
import concourse.mybir as mybir
from concourse.bass_utils import run_bass_kernel_spmd

F32 = mybir.dt.float32
BF16 = mybir.dt.bfloat16
AF = mybir.ActivationFunctionType
ALU = mybir.AluOpType
AX = mybir.AxisListType

L = 4096
C = 1024
H = 16
D = 64
NT = L // 128
NP = H // 2
SQT = 512
NST = L // SQT
RMS_EPS = 1e-6
LN_EPS = 1e-5
ISQ2 = 0.7071067811865476

ENGINES = ("sync", "tensor", "vector", "scalar", "gpsimd")


def mk(method, *args, **kw):
    def f(eng):
        return getattr(eng, method)(*args, **kw)
    return f


class Plan:
    def __init__(self, nc, sems):
        self.nc = nc
        self.sems = sems
        self.cnt = {k: 0 for k in sems}
        self.ops = {e: [] for e in ENGINES}
        self.chain = {}

    def op(self, eng, fn, wait=(), inc=()):
        for s, n in inc:
            self.cnt[s] += n
        self.ops[eng].append([list(wait), fn, list(inc)])

    def val(self, s):
        return self.cnt[s]

    def after(self, eng, s, n=1):
        assert self.ops[eng], f"no op on {eng}"
        self.ops[eng][-1][2].append((s, n))
        self.cnt[s] += n
        return self.cnt[s]

    CHAIN = {"vector": "sDV", "scalar": "sSC", "gpsimd": "sGP"}

    def sop(self, eng, fn, wait=()):
        cs = self.CHAIN[eng]
        w = list(wait)
        if self.chain.get(eng):
            w.append((cs, self.chain[eng]))
        self.op(eng, fn, wait=w)
        self.chain[eng] = self.after(eng, cs)
        return self.chain[eng]

    def barrier(self):
        dma_waits = [(s, self.cnt[s]) for s in
                     ("sIN", "sWL", "sX", "sQL", "sXL", "sSP", "sMN", "sST",
                      "sOUT", "sQP") if self.cnt[s] > 0]
        self.op("sync", mk("nop"), wait=dma_waits, inc=(("sB", 1),))
        self.op("gpsimd", mk("engine_nop"), wait=dma_waits, inc=(("sB", 1),))
        self.op("vector", mk("engine_nop"), inc=(("sB", 1),))
        self.op("scalar", mk("copy", self.bar_sb[:, 1:2], self.bar_sb[:, 0:1]),
                wait=list(self.bar_wait), inc=(("sB", 1),))
        stage1 = self.cnt["sB"]
        self.op("tensor", mk("transpose", self.bar_ps, self.bar_in,
                             self.bar_in),
                wait=list(self.bar_wait) + [("sB", stage1)],
                inc=(("sB", 1),))
        tgt = self.cnt["sB"]
        for e in ENGINES:
            self.ops[e].append([[("sB", tgt)], None, []])

    def flush(self):
        with self.nc.Block() as block:
            for ename in ENGINES:
                steps = self.ops[ename]
                if not steps:
                    continue

                def section(eng, steps=steps):
                    for waits, fn, incs in steps:
                        for s, v in waits:
                            if v > 0:
                                eng.wait_ge(self.sems[s], v)
                        if fn is None:
                            continue
                        ins = fn(eng)
                        for s, n in incs:
                            ins = ins.then_inc(self.sems[s], n)

                getattr(block, ename)(section)
        self.ops = {e: [] for e in ENGINES}


def build_program(phase_limit=99, dbg=False, interleave=True, parity=True):
    nc = bass.Bass("TRN2", target_bir_lowering=False,
                   detect_race_conditions=False)
    dti = lambda n, s, d: nc.dram_tensor(n, s, d, kind="ExternalInput").ap()

    x_d = dti("x", [L, C], F32)
    wqkvT_d = dti("wqkvT", [C, 3 * C], F32)
    bqkv_d = dti("bqkv", [1, 3 * C], F32)
    woutT_d = dti("woutT", [C, C], F32)
    bout_d = dti("bout", [1, C], F32)
    gq_d = dti("gq", [128, NP], F32)
    gk_d = dti("gk", [128, NP], F32)
    cos_d = dti("cosr", [L, 512], F32)
    sin_d = dti("sinr", [L, 512], F32)
    cw1_d = dti("cw1", [C, C], F32)
    cw2_d = dti("cw2", [C, 512], F32)
    cw3_d = dti("cw3", [512, 256], F32)
    cw4_d = dti("cw4", [256, 1], F32)
    cb1_d = dti("cb1", [128, 8], F32)
    cb2_d = dti("cb2", [128, 4], F32)
    cb3_d = dti("cb3", [128, 2], F32)
    cb4_d = dti("cb4", [1, 1], F32)
    gw1_d = dti("gw1", [C, 512], F32)
    gw2_d = dti("gw2", [512, 256], F32)
    gw3_d = dti("gw3", [256, 1], F32)
    gb1_d = dti("gb1", [128, 4], F32)
    gb2_d = dti("gb2", [128, 2], F32)
    gb3_d = dti("gb3", [1, 1], F32)
    ar_d = dti("ar", [1, 1], F32)

    y_d = nc.dram_tensor("y", [L, C], F32, kind="ExternalOutput").ap()
    ik = "ExternalOutput" if dbg else "Internal"
    if dbg:
        dbg_d = nc.dram_tensor("dbgpre", [L, C], F32, kind="ExternalOutput").ap()
        dbgo_d = nc.dram_tensor("dbgotn", [NST, 128, 4096], BF16, kind="ExternalOutput").ap()
        dbgcc_d = nc.dram_tensor("dbgcc", [24 * 128, 8], F32, kind="ExternalOutput").ap()
        dbgob_d = nc.dram_tensor("dbgob", [24 * 128, 4096], BF16, kind="ExternalOutput").ap()
        dbgh1_d = nc.dram_tensor("dbgh1", [24 * 128, 4096], BF16, kind="ExternalOutput").ap()
        dbgh2_d = nc.dram_tensor("dbgh2", [24 * 128, 2048], BF16, kind="ExternalOutput").ap()
        dbgh3_d = nc.dram_tensor("dbgh3", [24 * 128, 1024], BF16, kind="ExternalOutput").ap()
        dbgtf_d = nc.dram_tensor("dbgtf", [24, 512], F32, kind="ExternalOutput").ap()
        dbgcab_d = nc.dram_tensor("dbgcab", [128, 2], F32, kind="ExternalOutput").ap()
    qsp_d = nc.dram_tensor("qsp", [NT, 128, 128 * NP], BF16, kind=ik).ap()
    str_d = nc.dram_tensor("strm", [L, C], F32, kind=ik).ap()
    strb_d = nc.dram_tensor("strmb", [L, C], BF16, kind="Internal").ap()
    ssp_d = nc.dram_tensor("ssp", [L, C], F32, kind=ik).ap()

    es = contextlib.ExitStack()
    sbt = lambda n, s, d: es.enter_context(nc.sbuf_tensor(n, s, d))
    sem_names = ["sB", "sIN", "sWL", "sX", "sCS", "sTP", "sEVX", "sMM", "sVT",
                 "sRP", "sQS", "sQL", "sPE", "sEX", "sPV", "sEV", "sTR", "sDV",
                 "sXL", "sSP", "sOUT", "sSC", "sMN", "sGP", "sST", "sQP"]
    sems = {n: es.enter_context(nc.semaphore(n)) for n in sem_names}
    P = Plan(nc, sems)

    identb = sbt("identb", [128, 128], BF16)
    identf = sbt("identf", [128, 128], F32)
    onesr = sbt("onesr", [1, 128], BF16)
    onesc = sbt("onesc", [128, 1], BF16)
    onesrf = sbt("onesrf", [1, 128], F32)
    macc = sbt("macc", [128, 1024], F32)
    gpc = sbt("gpc", [128, NP], F32)
    BIAS = sbt("BIAS", [128, NP], F32)
    epsc = sbt("epsc", [128, 4], F32)
    qsmp = sbt("qsmp", [128, NP * 64], BF16)
    cabc = sbt("cabc", [128, 2], F32)
    gqc = sbt("gqc", [128, NP], F32)
    gkc = sbt("gkc", [128, NP], F32)
    barx = sbt("barx", [128, 2], F32)
    es2 = contextlib.ExitStack()
    vt = es2.enter_context(nc.sbuf_tensor("vt", [128, NT * H * 65], BF16))
    kT = es2.enter_context(nc.sbuf_tensor("kT", [128, NP * L], BF16))
    PSA = es.enter_context(nc.psum_tensor("PSA", [128, 4096], F32))
    PSAb = PSA[:].bitcast(BF16)
    P.bar_sb = barx
    P.bar_ps = PSAb[0:1, 4094:4095]
    P.bar_in = identb[0:1, 0:1]
    vt4 = vt[:].rearrange("p (t h c) -> p t h c", t=NT, h=H)


    def _stub(close_es2):
        if close_es2:
            es2.close()
        with contextlib.ExitStack() as ph2:
            yts = ph2.enter_context(nc.sbuf_tensor("ytstub", [128, C], F32))
            P.op("gpsimd", mk("memset", yts[:], 0.0))
            mz = P.after("gpsimd", "sGP")
            for t in range(NT):
                P.op("sync", mk("dma_start", out=y_d[128 * t:128 * t + 128, :],
                                in_=yts[:]),
                     wait=[("sGP", mz)] if t == 0 else [],
                     inc=(("sOUT", 16),))
            P.op("sync", mk("nop"), wait=[("sOUT", P.val("sOUT"))])
            P.flush()
        es.close()
        return nc

    P.sop("gpsimd", mk("memset", barx[:], 0.0))
    P.sop("gpsimd", mk("memset", identb[:], 0.0))
    P.sop("gpsimd", mk("affine_select", out=identb[:], in_=identb[:],
                       compare_op=ALU.not_equal, fill=1.0, base=0,
                       pattern=[[-1, 128]], channel_multiplier=1))
    P.bar_wait = [("sGP", P.val("sGP"))]
    P.sop("gpsimd", mk("memset", identf[:], 0.0))
    P.sop("gpsimd", mk("affine_select", out=identf[:], in_=identf[:],
                       compare_op=ALU.not_equal, fill=1.0, base=0,
                       pattern=[[-1, 128]], channel_multiplier=1))
    P.op("gpsimd", mk("memset", onesr[:], 1.0))
    P.op("gpsimd", mk("memset", onesc[:], 1.0))
    P.op("gpsimd", mk("memset", onesrf[:], 1.0))
    P.op("gpsimd", mk("memset", epsc[:, 0:1], RMS_EPS))
    P.op("gpsimd", mk("memset", epsc[:, 1:2], float(np.log(0.125))))
    P.op("gpsimd", mk("memset", epsc[:, 2:3], LN_EPS))
    P.op("gpsimd", mk("memset", vt4[:, :, :, 64:65], 1.0))
    P.op("sync", mk("dma_start", out=gqc[:], in_=gq_d[:]), inc=(("sIN", 16),))
    P.op("sync", mk("dma_start", out=gkc[:], in_=gk_d[:]), inc=(("sIN", 16),))
    P.op("vector", mk("tensor_mul", gpc[:], gqc[:], gkc[:]),
         wait=[("sIN", P.val("sIN"))])
    P.barrier()
    P.flush()
    if phase_limit <= 1:
        return _stub(True)

    def projection_pass(part):
        with contextlib.ExitStack() as ph:
            psb = lambda n, s, d: ph.enter_context(nc.sbuf_tensor(n, s, d))
            W = psb(f"W{part}", [128, 8 * C], BF16)
            brow = psb(f"br{part}", [1, C], BF16)
            xb = psb(f"xb{part}", [128, 2 * C], BF16)
            xT = psb(f"xT{part}", [128, 2 * C], BF16)
            drnp = psb(f"drn{part}", [128, 64], F32)
            col0 = {"q": 0, "k": C, "v": 2 * C}[part]
            for cc in range(8):
                P.op("gpsimd", mk("dma_start", out=W[:, 1024 * cc:1024 * cc + 1024],
                                  in_=wqkvT_d[128 * cc:128 * cc + 128, col0:col0 + C]),
                     inc=(("sWL", 16),))
            P.op("gpsimd", mk("dma_start", out=brow[:],
                              in_=bqkv_d[:, col0:col0 + C]), inc=(("sWL", 16),))
            w_ready = P.val("sWL")

            if part != "v":
                qf = psb(f"qf{part}", [128, 2 * C], F32)
                qb = psb(f"qb{part}", [128, 2 * C], BF16)
                cst = psb(f"cst{part}", [128, 2048], F32)
                rt = psb(f"rt{part}", [128, 1024], F32)
                rt2 = psb(f"rt2{part}", [128, 1024], F32)
                sqb = psb(f"sq{part}", [128, 1024], F32)
                sums = psb(f"sm{part}", [128, 96], F32)
                if part == "q":
                    qTst = psb("qTst", [128, 2 * C], BF16)

            pacc2 = (PSA[:, 0:1024], PSA[:, 1024:2048])
            ptX = PSAb[:, 4096:5120]
            ptQ = PSAb[:, 6144:7168]
            Z = lambda: [0] * (NT + 2)
            ev = {k: Z() for k in ("xl", "tp", "evx", "mm", "acc", "rp", "tp2",
                                   "qk", "cs", "spl", "qbd", "inv", "red",
                                   "gp3", "gp4", "csl")}

            def stageA(t):
                bi = t % 2
                w = []
                if t >= 2:
                    w.append(("sTP", ev["tp"][t - 2]))
                P.op("gpsimd", mk("dma_start", out=xb[:, C * bi:C * bi + C],
                                  in_=x_d[128 * t:128 * t + 128, :]),
                     wait=w, inc=(("sX", 16),))
                ev["xl"][t] = P.val("sX")
                if part != "v":
                    w = [("sCS", ev["cs"][t - 2])] if t >= 2 else []
                    P.op("sync", mk("dma_start", out=cst[:, 1024 * bi:1024 * bi + 512],
                                    in_=cos_d[128 * t:128 * t + 128, :]),
                         wait=w, inc=(("sIN", 16),))
                    P.op("sync", mk("dma_start",
                                    out=cst[:, 1024 * bi + 512:1024 * bi + 1024],
                                    in_=sin_d[128 * t:128 * t + 128, :]),
                         inc=(("sIN", 16),))
                    ev["csl"][t] = P.val("sIN")
                for cc in range(8):
                    w = []
                    if cc == 0:
                        w.append(("sX", ev["xl"][t]))
                        if t > 0:
                            w.append(("sEVX", ev["evx"][t - 1]))
                    P.op("tensor", mk("transpose", ptX[:, 128 * cc:128 * cc + 128],
                                      xb[:, C * bi + 128 * cc:C * bi + 128 * cc + 128],
                                      identb[:]), wait=w)
                ev["tp"][t] = P.after("tensor", "sTP")
                w = [("sTP", ev["tp"][t])]
                if t >= 2:
                    w.append(("sMM", ev["mm"][t - 2]))
                P.op("scalar", mk("copy", xT[:, C * bi:C * bi + C],
                                  ptX[:]), wait=w)
                P.op("scalar", mk("copy", drnp[:], identf[:, 0:64]))
                ev["evx"][t] = P.after("scalar", "sEVX")
                pacc = pacc2[bi]
                for cc in range(8):
                    for n in range(2):
                        w = []
                        if cc == 0 and n == 0:
                            w.append(("sEVX", ev["evx"][t]))
                            w.append(("sWL", w_ready))
                            if t >= 2:
                                w.append(("sVT", ev["acc"][t - 2]) if part == "v"
                                         else ("sRP", ev["rp"][t - 2]))
                        P.op("tensor", mk("matmul", pacc[:, 512 * n:512 * n + 512],
                                          xT[:, C * bi + 128 * cc:C * bi + 128 * cc + 128],
                                          W[:, 1024 * cc + 512 * n:1024 * cc + 512 * n + 512],
                                          start=(cc == 0), stop=False,
                                          skip_group_check=True), wait=w)
                for n in range(2):
                    P.op("tensor", mk("matmul", pacc[:, 512 * n:512 * n + 512],
                                      onesr[:], brow[:, 512 * n:512 * n + 512],
                                      start=False, stop=True,
                                      skip_group_check=True))
                ev["mm"][t] = P.after("tensor", "sMM")

            def stageB(t):
                bi = t % 2
                pacc = pacc2[bi]
                if part == "v":
                    P.op("vector", mk("tensor_copy", vt4[:, t, :, 0:64],
                                      pacc[:].rearrange("p (h z) -> p h z", h=H)),
                         wait=[("sMM", ev["mm"][t])])
                    ev["acc"][t] = P.after("vector", "sVT")
                    return
                cs_loaded = ev["csl"][t]
                w = [("sMM", ev["mm"][t])]
                if t >= 2:
                    w.append(("sDV", ev["qbd"][t - 2]))
                qfb = qf[:, C * bi:C * bi + C]
                P.op("scalar", mk("copy", qfb, pacc[:]), wait=w)
                ev["rp"][t] = P.after("scalar", "sRP")
                he = qfb.rearrange("p (h z) -> p h z", h=H)[:, :, 0:32]
                ho = qfb.rearrange("p (h z) -> p h z", h=H)[:, :, 32:64]
                ce = cst[:, 1024 * bi:1024 * bi + 512].rearrange("p (h z) -> p h z", h=H)
                se = cst[:, 1024 * bi + 512:1024 * bi + 1024].rearrange("p (h z) -> p h z", h=H)
                t1 = rt[:, 0:512].rearrange("p (h z) -> p h z", h=H)
                t2 = rt[:, 512:1024].rearrange("p (h z) -> p h z", h=H)
                t3 = rt2[:, 0:512].rearrange("p (h z) -> p h z", h=H)
                t4 = rt2[:, 512:1024].rearrange("p (h z) -> p h z", h=H)
                w = [("sIN", cs_loaded), ("sRP", ev["rp"][t])]
                if t >= 1:
                    w.append(("sCS", ev["cs"][t - 1]))
                P.op("gpsimd", mk("tensor_mul", t3, he, se), wait=w)
                ev["gp3"][t] = P.after("gpsimd", "sGP")
                P.op("gpsimd", mk("tensor_mul", t4, ho, ce))
                ev["gp4"][t] = P.after("gpsimd", "sGP")
                P.op("vector", mk("tensor_mul", t1, he, ce),
                     wait=[("sIN", cs_loaded), ("sRP", ev["rp"][t])])
                P.op("vector", mk("tensor_mul", t2, ho, se))
                P.op("vector", mk("tensor_mul", sqb[:], qfb, qfb))
                P.op("vector", mk("tensor_sub", he, t1, t2),
                     wait=[("sGP", ev["gp3"][t])])
                P.op("vector", mk("tensor_add", ho, t3, t4),
                     wait=[("sGP", ev["gp4"][t])])
                ev["cs"][t] = P.after("vector", "sCS")
                sums_done = P.sop("vector", mk("tensor_reduce",
                                               sums[:, 48 * bi:48 * bi + 16],
                                               sqb[:].rearrange("p (h z) -> p h z", h=H),
                                               AX.X, ALU.add))
                ev["red"][t] = sums_done
                P.sop("scalar", mk("activation", sums[:, 48 * bi + 16:48 * bi + 32],
                                   sums[:, 48 * bi:48 * bi + 16],
                                   AF.Ln, bias=epsc[:, 0:1], scale=1.0 / 64.0),
                      wait=[("sDV", sums_done)])
                if part == "q":
                    ev["inv"][t] = P.sop("scalar",
                                         mk("activation",
                                            sums[:, 48 * bi + 32:48 * bi + 48],
                                            sums[:, 48 * bi + 16:48 * bi + 32],
                                            AF.Exp, bias=epsc[:, 1:2],
                                            scale=-0.5))
                else:
                    ev["inv"][t] = P.sop("scalar",
                                         mk("activation",
                                            sums[:, 48 * bi + 32:48 * bi + 48],
                                            sums[:, 48 * bi + 16:48 * bi + 32],
                                            AF.Exp, scale=-0.5))

            def stageC(t):
                if part == "v":
                    return
                bi = t % 2
                qfb = qf[:, C * bi:C * bi + C]
                qbb = qb[:, C * bi:C * bi + C]
                for h in range(H):
                    w = []
                    if h == 0:
                        w.append(("sSC", ev["inv"][t]))
                        if t >= 2:
                            w.append(("sTP", ev["tp2"][t - 2]))
                    P.op("vector", mk("tensor_scalar", qbb[:, 64 * h:64 * h + 64],
                                      qfb[:, 64 * h:64 * h + 64],
                                      sums[:, 48 * bi + 32 + h:48 * bi + 33 + h],
                                      None, ALU.mult),
                         wait=w)
                P.op("vector", mk("tensor_copy", sqb[:, 0:128], sqb[:, 128:256]))
                ev["qbd"][t] = P.after("vector", "sDV")
                for cc in range(8):
                    w = []
                    if cc == 0:
                        w.append(("sDV", ev["qbd"][t]))
                        if t > 0:
                            w.append(("sQS", ev["qk"][t - 1]))
                    P.op("tensor", mk("transpose", ptQ[:, 128 * cc:128 * cc + 128],
                                      qbb[:, 128 * cc:128 * cc + 128], identb[:]),
                         wait=w)
                ev["tp2"][t] = P.after("tensor", "sTP")
                if part == "q":
                    w = [("sTP", ev["tp2"][t])]
                    if t >= 2:
                        w.append(("sQP", ev["spl"][t - 2]))
                    P.op("scalar", mk("copy", qTst[:, C * bi:C * bi + C],
                                      ptQ[:]), wait=w)
                    qtc = P.after("scalar", "sSC")
                    smp_dst = qsmp[:].rearrange("p (j s) -> p j s", j=NP)[:, :, 2 * t:2 * t + 2]
                    smp_src = qTst[:, C * bi:C * bi + C].rearrange("p (j s) -> p j s", j=NP)[:, :, 0:128:64]
                    P.op("vector", mk("tensor_copy", smp_dst, smp_src),
                         wait=[("sSC", qtc)])
                    ev["qk"][t] = P.after("vector", "sQS")
                    P.op("scalar", mk("dma_start", out=qsp_d[t],
                                      in_=qTst[:, C * bi:C * bi + C]),
                         wait=[("sQS", ev["qk"][t])], inc=(("sQP", 16),))
                    ev["spl"][t] = P.val("sQP")
                else:
                    for jj in range(NP):
                        w = [("sTP", ev["tp2"][t])] if jj == 0 else []
                        P.op("vector", mk("tensor_scalar",
                                          kT[:, L * jj + 128 * t:L * jj + 128 * t + 128],
                                          ptQ[:, 128 * jj:128 * jj + 128],
                                          gpc[:, jj:jj + 1], None, ALU.mult),
                             wait=w)
                    ev["qk"][t] = P.after("vector", "sQS")

            for i in range(NT + 2):
                if i < NT:
                    stageA(i)
                if 0 <= i - 1 < NT:
                    stageB(i - 1)
                if 0 <= i - 2 < NT:
                    stageC(i - 2)
            P.barrier()
            P.flush()


    projection_pass("v")
    if phase_limit <= 2:
        return _stub(True)
    projection_pass("q")
    if phase_limit <= 3:
        return _stub(True)
    projection_pass("k")
    if phase_limit <= 4:
        return _stub(True)

    with contextlib.ExitStack() as ph:
        scr = ph.enter_context(nc.sbuf_tensor("smx", [128, 4], F32))
        nmb = ph.enter_context(nc.sbuf_tensor("nmb", [1, NP], BF16))
        psmpA = PSA[:, 0:64]
        psmpB = PSA[:, 512:576]
        ptr1 = PSA[0:1, 1536:1664]
        pbc = PSA[:, 1024:1024 + NP]
        prev_dv = 0
        prev_tr = 0
        for j in range(NP):
            kj = kT[:, L * j:L * (j + 1)]
            w = [("sDV", prev_dv)] if j else []
            P.op("tensor", mk("matmul", psmpA,
                              kj[0:64, 0:L:32], qsmp[0:64, 64 * j:64 * j + 64],
                              start=True, stop=True, tile_position=(0, 0),
                              skip_group_check=True), wait=w)
            P.op("tensor", mk("matmul", psmpB,
                              kj[64:128, 0:L:32], qsmp[64:128, 64 * j:64 * j + 64],
                              start=True, stop=True, tile_position=(64, 0),
                              skip_group_check=True))
            mm_done = P.after("tensor", "sPE")
            w = [("sPE", mm_done)]
            if j:
                w.append(("sPE", prev_tr))
            P.sop("vector", mk("reduce_max", scr[:, 0:1], psmpA, AX.X), wait=w)
            P.sop("vector", mk("reduce_max", scr[:, 1:2], psmpB, AX.X))
            col_done = P.sop("vector", mk("tensor_max", scr[:, 0:1],
                                          scr[:, 0:1], scr[:, 1:2]))
            P.op("tensor", mk("transpose", ptr1[:], scr[:, 0:1], identf[:]),
                 wait=[("sDV", col_done)])
            prev_tr = P.after("tensor", "sPE")
            P.sop("vector", mk("reduce_max", scr[0:1, 1:2], ptr1[:], AX.X),
                  wait=[("sPE", prev_tr)])
            prev_dv = P.sop("vector", mk("tensor_scalar", nmb[:, j:j + 1],
                                         scr[0:1, 1:2], -1.0, None, ALU.mult))
        P.op("tensor", mk("matmul", pbc[:], onesr[:], nmb[:],
                          start=True, stop=True, skip_group_check=True),
             wait=[("sDV", prev_dv)])
        bmm = P.after("tensor", "sPE")
        P.op("vector", mk("tensor_copy", BIAS[:], pbc[:]), wait=[("sPE", bmm)])
        P.barrier()
        P.flush()
    if phase_limit <= 5:
        return _stub(True)

    with contextlib.ExitStack() as ph:
        psb = lambda n, s, d: ph.enter_context(nc.sbuf_tensor(n, s, d))
        qbuf = psb("qbuf", [128, 2 * SQT], BF16)
        PT = psb("PT", [128, 2 * 1024], BF16)
        oraw = psb("oraw", [128, 2 * 1024], BF16)
        ostg = psb("ostg", [128, 2 * 512], BF16)
        invz = psb("invz", [128, 4], F32)
        oTn = psb("oTn", [128, 2 * 4096], BF16)
        wout = psb("wout", [128, 8192], BF16)
        borow = psb("borow", [1, C], BF16)
        xr = psb("xr", [128, C], F32)
        strm = psb("strmt", [128, C], F32)
        lncols = psb("lnc", [128, 8], F32)
        sqt_ = psb("sqt", [128, C], F32)

        psTb = (PSA[:, 0:1024], PSA[:, 1024:2048])
        if parity:
            poAp = (PSA[:, 2048:2560], PSA[:, 3072:3584])
            poBp = (PSA[:, 2560:3072], PSA[:, 3584:4096])
            oQbase = (4096, 6144)
            pout = PSA[:, 3072:4096]
        else:
            poAp = (PSA[:, 2048:2560],) * 2
            poBp = (PSA[:, 2560:3072],) * 2
            oQbase = (4096, 4096)
            pout = PSA[:, 3072:4096]

        for cc in range(8):
            P.op("gpsimd", mk("dma_start", out=wout[:, 1024 * cc:1024 * cc + 1024],
                              in_=woutT_d[128 * cc:128 * cc + 128, :]),
                 inc=(("sWL", 16),))
        P.op("gpsimd", mk("dma_start", out=borow[:], in_=bout_d[:]),
             inc=(("sWL", 16),))
        wout_ready = P.val("sWL")

        qload = {}

        def load_qhat(st, j, wait):
            b = (st * NP + j) % 2
            src = qsp_d[4 * st:4 * st + 4, :, 128 * j:128 * j + 128].transpose([1, 0, 2])
            dst = qbuf[:, SQT * b:SQT * b + SQT].rearrange("p (t z) -> p t z", t=4)
            P.op("sync", mk("dma_start", out=dst, in_=src), wait=wait,
                 inc=(("sQL", 16),))
            qload[(st, j)] = P.val("sQL")

        load_qhat(0, 0, [])
        load_qhat(0, 1, [])

        exp_done = {}
        pv_done = {}
        epi_dv = {}
        epi_parts = {}
        last_out_dv = 0

        def plan_epilogue_A(st, j):
            par = j % 2
            poA, poB = poAp[par], poBp[par]
            orw = oraw[:, 1024 * par:1024 * par + 1024]
            P.op("vector", mk("tensor_copy", orw[0:65, 0:512], poA[0:65, :]),
                 wait=[("sPV", pv_done[(st, j, 31)])])
            P.op("vector", mk("tensor_copy", orw[0:65, 512:1024], poB[0:65, :]))
            ev_done = P.after("vector", "sEV")
            base = oQbase[par]
            for q in range(4):
                w = [("sEV", ev_done)] if q == 0 else []
                P.op("tensor", mk("transpose",
                                  PSAb[:, base + 256 * q:base + 256 * q + 65],
                                  orw[0:65, 128 * q:128 * q + 128],
                                  identb[0:65, 0:65]), wait=w)
                P.op("tensor", mk("transpose",
                                  PSAb[:, base + 256 * q + 128:base + 256 * q + 193],
                                  orw[0:65, 512 + 128 * q:512 + 128 * q + 128],
                                  identb[0:65, 0:65]))
            tr1 = P.after("tensor", "sTR")
            stg = {}
            for q in range(4):
                oQA = PSAb[:, base + 256 * q:base + 256 * q + 128]
                oQB = PSAb[:, base + 256 * q + 128:base + 256 * q + 256]
                w = [("sTR", tr1)] if q == 0 else []
                P.sop("vector", mk("reciprocal", invz[:, 2 * par:2 * par + 1],
                                   oQA[:, 64:65]), wait=w)
                P.sop("vector", mk("reciprocal", invz[:, 2 * par + 1:2 * par + 2],
                                   oQB[:, 64:65]))
                P.sop("vector", mk("tensor_scalar",
                                   ostg[:, 512 * par + 128 * q:512 * par + 128 * q + 64],
                                   oQA[:, 0:64], invz[:, 2 * par:2 * par + 1],
                                   None, ALU.mult))
                stg[q] = P.sop("vector", mk("tensor_scalar",
                                            ostg[:, 512 * par + 128 * q + 64:512 * par + 128 * q + 128],
                                            oQB[:, 0:64], invz[:, 2 * par + 1:2 * par + 2],
                                            None, ALU.mult))
            return stg

        def plan_epilogue_B(st, j, stg):
            par = j % 2
            base = oQbase[par]
            for q in range(4):
                P.op("tensor", mk("transpose",
                                  PSAb[:, base + 1024 + 128 * q:base + 1024 + 128 * q + 128],
                                  ostg[:, 512 * par + 128 * q:512 * par + 128 * q + 128],
                                  identb[:]), wait=[("sDV", stg[q])])
            tr2_all = P.after("tensor", "sTR")
            last = 0
            ob_ = oTn[:, 4096 * (st % 2):4096 * (st % 2) + 4096]
            for q in range(4):
                oNN = PSAb[:, base + 1024 + 128 * q:base + 1024 + 128 * q + 128]
                P.op("vector", mk("tensor_copy",
                                  ob_[:, 1024 * q + 128 * j:1024 * q + 128 * j + 128],
                                  oNN), wait=[("sTR", tr2_all)] if q == 0 else [])
                last = P.after("vector", "sDV")
            epi_dv[(st, j)] = last

        def plan_outproj_qtile(ost, q, state):
            ob_ = oTn[:, 4096 * (ost % 2):4096 * (ost % 2) + 4096]
            for kj in range(8):
                for n in range(2):
                    w = []
                    if kj == 0 and n == 0:
                        if q == 0:
                            w += state["all_epi"]()
                            w.append(("sWL", wout_ready))
                        else:
                            w.append(("sDV", state["add_dvs"][q - 1]))
                    P.op("tensor", mk("matmul", pout[:, 512 * n:512 * n + 512],
                                      ob_[:, 1024 * q + 128 * kj:1024 * q + 128 * kj + 128],
                                      wout[:, 1024 * kj + 512 * n:1024 * kj + 512 * n + 512],
                                      start=(kj == 0), stop=False,
                                      skip_group_check=True), wait=w)
            for n in range(2):
                P.op("tensor", mk("matmul", pout[:, 512 * n:512 * n + 512],
                                  onesr[:], borow[:, 512 * n:512 * n + 512],
                                  start=False, stop=True,
                                  skip_group_check=True))
            op_done = P.after("tensor", "sPE")
            trow = SQT * ost + 128 * q
            w = [("sDV", P.val("sDV"))] if (ost or q) else []
            P.op("sync", mk("dma_start", out=xr[:],
                            in_=x_d[trow:trow + 128, :]), wait=w,
                 inc=(("sXL", 16),))
            x_loaded = P.val("sXL")
            w = [("sPE", op_done), ("sXL", x_loaded),
                 ("sSP", P.val("sSP"))]
            P.op("vector", mk("tensor_add", strm[:], pout[:], xr[:]), wait=w)
            state["add_dvs"][q] = P.after("vector", "sDV")
            if dbg:
                P.op("gpsimd", mk("dma_start",
                                  out=dbg_d[trow:trow + 128, :],
                                  in_=strm[:]), wait=[("sDV", state["add_dvs"][q])],
                     inc=(("sSP", 16),))
                P.op("vector", mk("engine_nop"),
                     wait=[("sSP", P.val("sSP"))])
            P.sop("vector", mk("reduce_sum", lncols[:, 0:1], strm[:], AX.X))
            P.sop("vector", mk("scalar_tensor_tensor", sqt_[:], strm[:], 1.0,
                               strm[:], ALU.mult, ALU.mult,
                               accum_out=lncols[:, 1:2]))
            P.sop("vector", mk("tensor_scalar", lncols[:, 2:3], lncols[:, 0:1],
                               1.0 / C, None, ALU.mult))
            P.sop("vector", mk("tensor_mul", lncols[:, 3:4],
                               lncols[:, 2:3], lncols[:, 2:3]))
            P.sop("vector", mk("tensor_scalar", lncols[:, 4:5], lncols[:, 1:2],
                               1.0 / C, None, ALU.mult))
            var_done = P.sop("vector", mk("tensor_sub", lncols[:, 5:6],
                                          lncols[:, 4:5], lncols[:, 3:4]))
            P.sop("scalar", mk("activation", lncols[:, 6:7], lncols[:, 5:6],
                               AF.Ln, bias=epsc[:, 2:3]),
                  wait=[("sDV", var_done)])
            rstd_done = P.sop("scalar", mk("activation", lncols[:, 7:8],
                                           lncols[:, 6:7], AF.Exp,
                                           scale=-0.5))
            P.sop("vector", mk("tensor_scalar", strm[:], strm[:],
                               lncols[:, 2:3], lncols[:, 7:8],
                               ALU.subtract, ALU.mult),
                  wait=[("sSC", rstd_done)])
            ln_d = P.val("sDV")
            P.op("gpsimd", mk("dma_start", out=str_d[trow:trow + 128, :],
                              in_=strm[:]), wait=[("sDV", ln_d)],
                 inc=(("sSP", 16),))
            P.op("gpsimd", mk("dma_start", out=strb_d[trow:trow + 128, :],
                              in_=strm[:]), wait=[("sDV", ln_d)],
                 inc=(("sSP", 16),))
            if ost == 0 and q == 0:
                P.sop("vector", mk("tensor_copy", macc[:], strm[:]))
            else:
                P.sop("vector", mk("tensor_add", macc[:], macc[:], strm[:]))

        pend_state = {}
        for st in range(NST):
            for j in range(NP):
                b = (st * NP + j) % 2
                par = j % 2
                poA, poB = poAp[par], poBp[par]
                pj = None if (st, j) == (0, 0) else \
                    ((st, j - 1) if j > 0 else (st - 1, NP - 1))
                prev2 = None if pj is None or pj == (0, 0) and (st, j) == (0, 1) \
                    else None
                i_lin = st * NP + j
                back = 2 if parity else 1
                if i_lin >= back:
                    pp = ((i_lin - back) // NP, (i_lin - back) % NP)
                else:
                    pp = None

                def emit_pv(c):
                    first = True
                    for a in range(2):
                        po = poA if a == 0 else poB
                        h = 2 * j + a
                        vsl = vt[:, 1040 * c + 65 * h:1040 * c + 65 * h + 65]
                        w = []
                        if first:
                            w.append(("sEX", exp_done[(st, j, c)]))
                            if c == 0:
                                if pp is not None:
                                    w.append(("sDV", epi_dv[pp]))
                                if (par == 1 or not parity) and last_out_dv:
                                    w.append(("sDV", last_out_dv))
                        first = False
                        P.op("tensor", mk("matmul", po[0:65, :], vsl,
                                          PT[:, 1024 * (c % 2) + 512 * a:
                                              1024 * (c % 2) + 512 * a + 512],
                                          start=(c == 0), stop=(c == 31),
                                          skip_group_check=True), wait=w)
                    pv_done[(st, j, c)] = P.after("tensor", "sPV")

                stg_prev = None
                for c in range(32):
                    w = []
                    if c == 0:
                        w.append(("sQL", qload[(st, j)]))
                    if c >= 2:
                        w.append(("sEX", exp_done[(st, j, c - 2)]))
                    elif pj is not None:
                        w.append(("sEX", exp_done[(pj[0], pj[1], 30 + c)]))
                    pb = psTb[c % 2]
                    P.op("tensor", mk("matmul", pb[:, 0:512],
                                      kT[0:64, L * j + 128 * c:L * j + 128 * c + 128],
                                      qbuf[0:64, SQT * b:SQT * b + 512],
                                      start=True, stop=True,
                                      tile_position=(0, 0),
                                      skip_group_check=True), wait=w)
                    P.op("tensor", mk("matmul", pb[:, 512:1024],
                                      kT[64:128, L * j + 128 * c:L * j + 128 * c + 128],
                                      qbuf[64:128, SQT * b:SQT * b + 512],
                                      start=True, stop=True,
                                      tile_position=(64, 0),
                                      skip_group_check=True))
                    st_done = P.after("tensor", "sPE")
                    w = [("sPE", st_done)]
                    if c >= 2:
                        w.append(("sPV", pv_done[(st, j, c - 2)]))
                    elif pj is not None:
                        w.append(("sPV", pv_done[(pj[0], pj[1], 30 + c)]))
                    P.op("scalar", mk("activation",
                                      PT[:, 1024 * (c % 2):1024 * (c % 2) + 1024],
                                      pb[:], AF.Exp, bias=BIAS[:, j:j + 1]),
                         wait=w)
                    exp_done[(st, j, c)] = P.after("scalar", "sEX")
                    if c >= 1:
                        emit_pv(c - 1)
                    if interleave and c == 2 and pj is not None and pj not in epi_dv:
                        stg_prev = plan_epilogue_A(pj[0], pj[1])
                    if interleave and c == 8 and pj is not None and stg_prev is not None and pj not in epi_dv:
                        plan_epilogue_B(pj[0], pj[1], stg_prev)
                        stg_prev = None
                    if j == 0 and st >= 1 and c in (10, 15, 20, 25):
                        q_o = (10, 15, 20, 25).index(c)
                        plan_outproj_qtile(st - 1, q_o, pend_state[st - 1])
                        if q_o == 3:
                            last_out_dv = pend_state[st - 1]["add_dvs"][3]
                    if c == 31:
                        nxt = (st + (j + 2) // NP, (j + 2) % NP)
                        if nxt[0] < NST:
                            load_qhat(nxt[0], nxt[1],
                                      [("sEX", exp_done[(st, j, 31)])])
                emit_pv(31)
                if (st, j) not in epi_dv and (not interleave or
                                               (j == NP - 1 and st == NST - 1)):
                    stg_last = plan_epilogue_A(st, j)
                    plan_epilogue_B(st, j, stg_last)

            pend_state[st] = {"all_epi": (lambda st=st:
                                          [("sDV", epi_dv[(st, jj)])
                                           for jj in range(NP)]),
                              "add_dvs": {}}
            if st == NST - 1:
                for q in range(4):
                    plan_outproj_qtile(st, q, pend_state[st])
                last_out_dv = pend_state[st]["add_dvs"][3]

        P.barrier()
        P.flush()
    es2.close()

    with contextlib.ExitStack() as ph:
        psb = lambda n, s, d: ph.enter_context(nc.sbuf_tensor(n, s, d))
        cw1 = psb("cw1s", [128, 8192], BF16)
        cw2 = psb("cw2s", [128, 4096], BF16)
        cw3 = psb("cw3s", [128, 1024], BF16)
        cw4 = psb("cw4s", [128, 2], BF16)
        cb1s = psb("cb1s", [128, 8], F32)
        cb2s = psb("cb2s", [128, 4], F32)
        cb3s = psb("cb3s", [128, 2], F32)
        cb4s = psb("cb4s", [1, 1], F32)
        gw1 = psb("gw1s", [128, 4096], BF16)
        gw2 = psb("gw2s", [128, 1024], BF16)
        gw3 = psb("gw3s", [128, 2], BF16)
        gb1s = psb("gb1s", [128, 4], F32)
        gb2s = psb("gb2s", [128, 2], F32)
        gb3s = psb("gb3s", [1, 1], F32)
        ars = psb("ars", [1, 1], F32)
        gmisc = psb("gmisc", [128, 16], F32)
        gcrow = psb("gcrow", [1, C], BF16)
        gcT = psb("gcT", [128, 8], BF16)
        gact = psb("gact", [128, 12], F32)
        gbf = psb("gbf", [128, 8], BF16)
        cabrow = psb("cabrow", [1, 2], F32)

        x1t = psb("x1t", [128, 2 * C], F32)
        onesmf = psb("onesmf", [128, 1], F32)
        outt = psb("outt", [128, 4 * C], F32)
        ob = psb("ob", [128, 4 * C], BF16)
        obT = psb("obT", [128, 4096], BF16)
        h1b = psb("h1b", [128, 4096], BF16)
        h2b = psb("h2b", [128, 2048], BF16)
        h3b = psb("h3b", [128, 1024], BF16)
        estg = psb("estg", [128, 512], F32)
        xh = psb("xh", [128, 512], F32)
        tfrow = psb("tfrow", [1, 512], F32)
        tfrow2 = psb("tfrow2", [1, 512], F32)
        tfT = psb("tfT", [128, 4], F32)
        drn = psb("drn", [128, 256], F32)
        obc = psb("obc", [128, 2 * 4096], BF16)
        ccols = psb("ccols", [128, 8], F32)
        twt = psb("twt", [128, 2 * C], F32)

        def wload(dst, src_d, kdim, mdim):
            for k in range(kdim // 128):
                P.op("gpsimd", mk("dma_start",
                                  out=dst[:, k * mdim:(k + 1) * mdim],
                                  in_=src_d[128 * k:128 * k + 128, :]),
                     inc=(("sWL", 16),))

        wload(cw1, cw1_d, C, C)
        wload(cw2, cw2_d, C, 512)
        wload(cw3, cw3_d, 512, 256)
        P.op("gpsimd", mk("dma_start", out=cw4[:, 0:1], in_=cw4_d[0:128, :]),
             inc=(("sWL", 16),))
        P.op("gpsimd", mk("dma_start", out=cw4[:, 1:2], in_=cw4_d[128:256, :]),
             inc=(("sWL", 16),))
        wload(gw1, gw1_d, C, 512)
        wload(gw2, gw2_d, 512, 256)
        P.op("gpsimd", mk("dma_start", out=gw3[:, 0:1], in_=gw3_d[0:128, :]),
             inc=(("sWL", 16),))
        P.op("gpsimd", mk("dma_start", out=gw3[:, 1:2], in_=gw3_d[128:256, :]),
             inc=(("sWL", 16),))
        for dst, src in ((cb1s, cb1_d), (cb2s, cb2_d), (cb3s, cb3_d),
                         (cb4s, cb4_d), (gb1s, gb1_d), (gb2s, gb2_d),
                         (gb3s, gb3_d), (ars, ar_d)):
            P.op("sync", mk("dma_start", out=dst[:], in_=src[:]),
                 inc=(("sIN", 16),))
        w_ready = P.val("sWL")
        b_ready = P.val("sIN")
        obc_ld_next = {}
        for tt in range(4):
            P.op("sync", mk("dma_start", out=obc[:, C * tt:C * tt + C],
                            in_=strb_d[128 * tt:128 * tt + 128, :]),
                 inc=(("sST", 16),))
            obc_ld_next[tt] = P.val("sST")

        pmean = PSA[:, 0:1024]
        ones_mf = P.sop("vector", mk("reduce_sum", onesmf[:], identf[:], AX.X))
        for n in range(2):
            P.op("tensor", mk("matmul", pmean[0:1, 512 * n:512 * n + 512],
                              onesmf[:], macc[:, 512 * n:512 * n + 512],
                              start=True, stop=True,
                              skip_group_check=True),
                 wait=[("sDV", ones_mf)] if n == 0 else [])
        mean_done = P.after("tensor", "sDV")
        P.op("scalar", mk("activation", gcrow[:], pmean[0:1, :], AF.Copy,
                          scale=1.0 / L), wait=[("sDV", mean_done)])
        gc_done = P.after("scalar", "sSC")
        pgt = PSAb[:, 2048:2080]
        for cc in range(8):
            P.op("tensor", mk("transpose", pgt[:, 2 * cc:2 * cc + 1],
                              gcrow[0:1, 128 * cc:128 * cc + 128],
                              identb[0:1, 0:1]),
                 wait=[("sSC", gc_done)] if cc == 0 else [])
        gct_done = P.after("tensor", "sPE")
        gct_sb = P.sop("vector", mk("tensor_copy", gcT[:], pgt[:, 0:16:2]),
                       wait=[("sPE", gct_done)])

        def gchain(win, nk, nm, rhs_ap, pdst_fn, wait):
            for m in range(nm):
                for k in range(nk):
                    w = list(wait) if (m == 0 and k == 0) else []
                    P.op("tensor", mk("matmul", pdst_fn(m),
                                      win[:, k * (nm * 128) + m * 128:
                                          k * (nm * 128) + m * 128 + 128],
                                      rhs_ap(k),
                                      start=(k == 0), stop=(k == nk - 1),
                                      skip_group_check=True), wait=w)
            return P.after("tensor", "sPE")

        pg1 = lambda m: PSA[:, 512 * m + 384:512 * m + 385]
        pg2 = lambda m: PSA[:, 2048 + 512 * m + 384:2048 + 512 * m + 385]
        g1_done = gchain(gw1, 8, 4, lambda k: gcT[:, k:k + 1], pg1,
                         [("sDV", gct_sb), ("sWL", w_ready)])
        for m in range(4):
            w = [("sPE", g1_done), ("sIN", b_ready)] if m == 0 else []
            P.op("scalar", mk("activation", gbf[:, m:m + 1], pg1(m),
                              AF.Gelu, bias=gb1s[:, m:m + 1]), wait=w)
        g1b = P.after("scalar", "sSC")
        g2_done = gchain(gw2, 4, 2, lambda k: gbf[:, k:k + 1], pg2,
                         [("sSC", g1b)])
        for m in range(2):
            w = [("sPE", g2_done)] if m == 0 else []
            P.op("scalar", mk("activation", gbf[:, 4 + m:5 + m],
                              pg2(m), AF.Gelu,
                              bias=gb2s[:, m:m + 1]), wait=w)
        g2b = P.after("scalar", "sSC")
        pgf = PSA[0:1, 3072 + 384:3072 + 385]
        for k in range(2):
            P.op("tensor", mk("matmul", pgf[:], gw3[:, k:k + 1],
                              gbf[:, 4 + k:5 + k], start=(k == 0), stop=(k == 1),
                              skip_group_check=True),
                 wait=[("sSC", g2b)] if k == 0 else [])
        gf_mm = P.after("tensor", "sPE")
        gfb = P.sop("vector", mk("tensor_add", gmisc[0:1, 0:1], pgf[:],
                                 gb3s[:]), wait=[("sPE", gf_mm)])
        th_done = P.sop("scalar", mk("activation", gmisc[0:1, 5:6],
                                     gmisc[0:1, 0:1], AF.Tanh, scale=0.5),
                        wait=[("sDV", gfb)])
        P.sop("vector", mk("tensor_scalar", gmisc[0:1, 1:2], gmisc[0:1, 5:6],
                           0.5, 0.5, ALU.mult, ALU.add),
              wait=[("sSC", th_done)])
        P.sop("vector", mk("tensor_mul", gmisc[0:1, 2:3], gmisc[0:1, 1:2],
                           ars[:]))
        P.sop("vector", mk("tensor_scalar", gmisc[0:1, 3:4], ars[:], -1.0, 0.1,
                           ALU.mult, ALU.add))
        P.sop("vector", mk("scalar_tensor_tensor", cabrow[:, 0:1],
                           gmisc[0:1, 2:3], 1.4, gmisc[0:1, 3:4],
                           ALU.mult, ALU.add))
        cabh = P.sop("vector", mk("tensor_scalar", cabrow[:, 1:2], ars[:],
                                  0.3, None, ALU.mult))
        cab_done = P.sop("vector", mk("tensor_add", cabrow[:, 0:1],
                                      cabrow[:, 0:1], cabrow[:, 1:2]))
        cb4h = gmisc[0:1, 6:7]
        cab_done = P.sop("vector", mk("tensor_scalar", cb4h, cb4s[:], 0.5,
                                      None, ALU.mult))
        pcab = PSA[:, 3584 + 128:3584 + 130]
        P.op("tensor", mk("matmul", pcab[:], onesrf[:], cabrow[:],
                          start=True, stop=True, skip_group_check=True),
             wait=[("sDV", cab_done)])
        cab_mm = P.after("tensor", "sPE")
        cabd = P.sop("vector", mk("tensor_copy", cabc[:], pcab[:]),
              wait=[("sPE", cab_mm)])
        if dbg:
            P.op("gpsimd", mk("dma_start", out=dbgcab_d[:], in_=cabc[:]),
                 wait=[("sDV", cabd)], inc=(("sSP", 16),))
        P.barrier()

        ph1 = (PSA[:, 0:512], PSA[:, 512:1024])
        ptile3s = (PSAb[:, 2048:3072], PSAb[:, 6144:7168])
        ptf = PSA[0:1, 1536:2048]
        _ptfT_cols = (2304, 2816, 3584, 1792)
        ptfT = lambda cc: PSA[:, _ptfT_cols[cc]:_ptfT_cols[cc] + 1]
        spill_prev = {tt: P.val("sSP") for tt in range(4)}
        add_hist = {}
        obc_tp_prev = {tt: 0 for tt in range(4)}
        g_tp = {}
        g_ob = {}
        pend_obT = {}

        def emit_tiles(s4v, cyc, w0_fn, src_fn, cr=None):
            obT_done = 0
            for tt in range(4):
                step = 4 * cyc + tt
                bi = step % 2
                pt3 = ptile3s[bi]
                src, scol = src_fn(tt)
                for cc in range(8):
                    w = []
                    if cc == 0:
                        w = w0_fn(tt)
                        key = (s4v, step - 2) if step >= 2 else                               (s4v - 1, step + 10)
                        if key in g_ob:
                            w.append(("sDV", g_ob[key]))
                    P.op("tensor", mk("transpose",
                                      pt3[:, 128 * cc:128 * cc + 128],
                                      src[:, scol + 128 * cc:scol + 128 * cc + 128],
                                      identb[:]), wait=w)
                g_tp[(s4v, step)] = P.after("tensor", "sTP")
                if cyc == 0:
                    obc_tp_prev[tt] = g_tp[(s4v, step)]
                dst = obT[:].rearrange("p (c z) -> p c z", c=8)[:, :, 128 * tt:128 * tt + 128]
                srcv = pt3[:].rearrange("p (c z) -> p c z", c=8)
                P.op("vector", mk("tensor_copy", dst, srcv),
                     wait=[("sTP", g_tp[(s4v, step)])])
                obT_done = P.after("vector", "sDV")
                g_ob[(s4v, step)] = obT_done
            return obT_done

        for s4 in range(8):
            obc_ld = dict(obc_ld_next)
            slot = (s4 % 2) * 4096
            if s4 < 7:
                nslot = ((s4 + 1) % 2) * 4096
                for tt in range(4):
                    t = 4 * (s4 + 1) + tt
                    w = []
                    if obc_tp_prev[tt]:
                        w.append(("sTP", obc_tp_prev[tt]))
                    P.op("sync", mk("dma_start",
                                    out=obc[:, nslot + C * tt:nslot + C * tt + C],
                                    in_=strb_d[128 * t:128 * t + 128, :]),
                         wait=w, inc=(("sST", 16),))
                    obc_ld_next[tt] = P.val("sST")
            ld_done = {}
            for tt in range(4):
                t = 4 * s4 + tt
                w = [("sSP", spill_prev[tt])]
                P.op("sync", mk("dma_start", out=outt[:, C * tt:C * tt + C],
                                in_=str_d[128 * t:128 * t + 128, :]), wait=w,
                     inc=(("sST", 16),))
                ld_done[tt] = P.val("sST")
            tw_done = {tt: 0 for tt in range(4)}
            copy_rdy = {}
            for cyc in range(3):
                if cyc == 0 and s4 in pend_obT:
                    obT_done = pend_obT.pop(s4)
                elif cyc == 0:
                    obT_done = emit_tiles(
                        s4, 0,
                        lambda tt: [("sST", obc_ld[tt])],
                        lambda tt: (obc, slot + C * tt))
                else:
                    obT_done = emit_tiles(
                        s4, cyc,
                        lambda tt: [("sSC", copy_rdy[tt])],
                        lambda tt: (ob, C * tt))

                def layer(win, bias, nk, nm, rhs, dst_b, wait_first):
                    gel = {}
                    for m in range(nm):
                        pb = ph1[m % 2]
                        for k in range(nk):
                            w = []
                            if k == 0:
                                if m == 0:
                                    w += wait_first
                                if m >= 2:
                                    w.append(("sSC", gel[m - 2]))
                            P.op("tensor", mk("matmul", pb[:],
                                              win[:, k * (nm * 128) + m * 128:
                                                  k * (nm * 128) + m * 128 + 128],
                                              rhs[:, 512 * k:512 * k + 512],
                                              start=(k == 0), stop=(k == nk - 1),
                                              skip_group_check=True), wait=w)
                        mmd = P.after("tensor", "sMM")
                        P.op("scalar", mk("activation",
                                          dst_b[:, 512 * m:512 * m + 512],
                                          pb[:], AF.Gelu,
                                          bias=bias[:, m:m + 1]),
                             wait=[("sMM", mmd)])
                        gel[m] = P.after("scalar", "sSC")
                    return gel[nm - 1]

                d1 = layer(cw1, cb1s, 8, 8, obT, h1b,
                           [("sDV", obT_done), ("sWL", w_ready),
                            ("sIN", b_ready)])
                if dbg:
                    r0 = (s4 * 3 + cyc) * 128
                    P.op("gpsimd", mk("dma_start",
                                      out=dbgob_d[r0:r0 + 128, :],
                                      in_=obT[:]),
                         wait=[("sDV", obT_done)], inc=(("sSP", 16),))
                    P.op("gpsimd", mk("dma_start",
                                      out=dbgh1_d[r0:r0 + 128, :],
                                      in_=h1b[:]),
                         wait=[("sSC", d1)], inc=(("sSP", 16),))
                d2 = layer(cw2, cb2s, 8, 4, h1b, h2b, [("sSC", d1)])
                if dbg:
                    P.op("gpsimd", mk("dma_start",
                                      out=dbgh2_d[r0:r0 + 128, :],
                                      in_=h2b[:, 0:2048]),
                         wait=[("sSC", d2)], inc=(("sSP", 16),))
                d3 = layer(cw3, cb3s, 4, 2, h2b, h3b, [("sSC", d2)])
                if dbg:
                    P.op("gpsimd", mk("dma_start",
                                      out=dbgh3_d[r0:r0 + 128, :],
                                      in_=h3b[:, 0:1024]),
                         wait=[("sSC", d3)], inc=(("sSP", 16),))
                for k in range(2):
                    w = []
                    if k == 0:
                        w = [("sSC", d3)]
                    P.op("tensor", mk("matmul", ptf[:], cw4[:, k:k + 1],
                                      h3b[:, 512 * k:512 * k + 512],
                                      start=(k == 0), stop=(k == 1),
                                      skip_group_check=True), wait=w)
                h4_done = P.after("tensor", "sMM")
                P.op("scalar", mk("activation", tfrow[:], ptf[:], AF.Tanh,
                                  bias=cb4h, scale=0.5), wait=[("sMM", h4_done)])
                tf_done = P.after("scalar", "sSC")
                P.op("vector", mk("tensor_copy", tfrow2[:], tfrow[:]),
                     wait=[("sSC", tf_done)])
                P.op("vector", mk("tensor_copy", drn[:, 0:128], identb[:]))
                tf2_done = P.after("vector", "sDV")
                for cc in range(4):
                    P.op("tensor", mk("transpose", ptfT(cc),
                                      tfrow2[0:1, 128 * cc:128 * cc + 128],
                                      identf[0:1, 0:1]),
                         wait=[("sDV", tf2_done)] if cc == 0 else [])
                tfT_mm = P.after("tensor", "sMM")
                for cc in range(4):
                    P.sop("vector", mk("tensor_scalar", ccols[:, cc:cc + 1],
                                       ptfT(cc), cabc[:, 1:2], cabc[:, 0:1],
                                       ALU.mult, ALU.add),
                          wait=[("sMM", tfT_mm)] if cc == 0 else [])
                cc_done = P.sop("vector", mk("tensor_scalar", ccols[:, 4:8],
                                             ccols[:, 0:4], -1.0, None,
                                             ALU.mult))
                if dbg:
                    r0 = (s4 * 3 + cyc) * 128
                    P.op("gpsimd", mk("dma_start",
                                      out=dbgcc_d[r0:r0 + 128, :],
                                      in_=ccols[:]),
                         wait=[("sDV", cc_done)], inc=(("sSP", 16),))
                    P.op("gpsimd", mk("dma_start",
                                      out=dbgtf_d[s4 * 3 + cyc:s4 * 3 + cyc + 1, :],
                                      in_=tfrow2[:]), inc=(("sSP", 16),))
                def prods(tt, w):
                    o = outt[:, C * tt:C * tt + C]
                    tw = twt[:, 1024 * (tt % 2):1024 * (tt % 2) + 1024]
                    cp = ccols[:, tt:tt + 1]
                    cn = ccols[:, 4 + tt:5 + tt]
                    P.sop("scalar", mk("mul", tw[:, 0:256], o[:, 512:768],
                                       cp), wait=w)
                    P.sop("scalar", mk("mul", tw[:, 256:512], o[:, 768:1024],
                                       cn))
                    P.sop("scalar", mk("mul", tw[:, 512:768], o[:, 0:256],
                                       cn))
                    return P.sop("scalar", mk("mul", tw[:, 768:1024],
                                              o[:, 256:512], cp))

                def dvadd(tt, pr):
                    o = outt[:, C * tt:C * tt + C]
                    tw = twt[:, 1024 * (tt % 2):1024 * (tt % 2) + 1024]
                    tw_done[tt] = ("sDV",
                                   P.sop("vector", mk("tensor_add", o, o,
                                                      tw[:]),
                                         wait=[("sSC", pr)]))

                def obcopy(tt):
                    P.sop("scalar", mk("copy", ob[:, C * tt:C * tt + C],
                                       outt[:, C * tt:C * tt + C]),
                          wait=[tw_done[tt]])
                    return P.sop("scalar", mk("copy", drn[:, 0:128],
                                              identf[:, 0:128]))

                if cyc == 2 and s4 < 7:
                    nsl = ((s4 + 1) % 2) * 4096
                    pend_obT[s4 + 1] = emit_tiles(
                        s4 + 1, 0,
                        lambda tt: [("sST", obc_ld_next[tt])],
                        lambda tt: (obc, nsl + C * tt))
                w0 = [("sDV", cc_done)]
                if cyc == 0:
                    w0 += [("sST", ld_done[tt2]) for tt2 in range(4)]
                pr = {0: prods(0, w0)}
                pr[1] = prods(1, [])
                for tt in range(4):
                    dvadd(tt, pr[tt])
                    if cyc < 2:
                        copy_rdy[tt] = obcopy(tt)
                    if tt + 2 < 4:
                        w = [] if cyc < 2 else [tw_done[tt]]
                        pr[tt + 2] = prods(tt + 2, w)
                P.op("tensor", mk("ldweights", identb[:]),
                     wait=[tw_done[0]])

            for tt in range(4):
                t = 4 * s4 + tt
                bi = tt % 2
                w = [("sGP", add_hist[tt - 2])] if tt >= 2 else []
                P.op("sync", mk("dma_start", out=x1t[:, C * bi:C * bi + C],
                                in_=str_d[128 * t:128 * t + 128, :]),
                     wait=w, inc=(("sST", 16),))
                x1_ld = P.val("sST")
                P.op("gpsimd", mk("tensor_add", outt[:, C * tt:C * tt + C],
                                  outt[:, C * tt:C * tt + C],
                                  x1t[:, C * bi:C * bi + C]),
                     wait=[("sST", x1_ld), tw_done[tt]])
                add_hist[tt] = P.after("gpsimd", "sGP")
                P.op("gpsimd", mk("dma_start",
                                  out=ssp_d[128 * t:128 * t + 128, :],
                                  in_=outt[:, C * tt:C * tt + C]),
                     wait=[("sGP", add_hist[tt])], inc=(("sSP", 16),))
                spill_prev[tt] = P.val("sSP")
        P.barrier()
        P.flush()
    if phase_limit <= 7:
        return _stub(False)

    with contextlib.ExitStack() as ph:
        psb = lambda n, s, d: ph.enter_context(nc.sbuf_tensor(n, s, d))
        sin_ = psb("sin", [128, 4 * C], F32)
        sq2 = psb("sq2", [128, C], F32)
        yt = psb("yt", [128, 2 * C], F32)
        lnc = psb("lnc2", [128, 16], F32)
        norm_done = {}
        ld_val = {}
        def p3b_load(tp, w):
            qeng, qsem = (("sync", "sST") if tp % 2 == 0 else
                          ("scalar", "sQP"))
            P.op(qeng, mk("dma_start", out=sin_[:, C * (tp % 4):C * (tp % 4) + C],
                          in_=ssp_d[128 * tp:128 * tp + 128, :]),
                 wait=w, inc=((qsem, 16),))
            ld_val[tp] = (qsem, P.val(qsem))

        st2v = {}

        def p3b_stats(t):
            bi8 = 8 * (t % 2)
            bfr = sin_[:, C * (t % 4):C * (t % 4) + C]
            w = [ld_val[t]]
            if t >= 2:
                w.append(("sDV", norm_done[t - 2]))
            P.sop("scalar", mk("activation", sq2[:], bfr, AF.Copy,
                               accum_out=lnc[:, bi8 + 0:bi8 + 1]),
                  wait=w)
            P.sop("scalar", mk("activation", sq2[:], bfr, AF.Square,
                               accum_out=lnc[:, bi8 + 1:bi8 + 2]))
            st2v[t] = P.sop("scalar", mk("copy", sq2[:, 0:128],
                                         sq2[:, 128:256]))

        def p3b_fin(t):
            bi8 = 8 * (t % 2)
            bfr = sin_[:, C * (t % 4):C * (t % 4) + C]
            P.sop("vector", mk("tensor_scalar", lnc[:, bi8 + 2:bi8 + 3],
                               lnc[:, bi8 + 0:bi8 + 1],
                               1.0 / C, None, ALU.mult),
                  wait=[("sSC", st2v[t])])
            P.sop("vector", mk("tensor_mul", lnc[:, bi8 + 3:bi8 + 4],
                               lnc[:, bi8 + 2:bi8 + 3],
                               lnc[:, bi8 + 2:bi8 + 3]))
            P.sop("vector", mk("tensor_scalar", lnc[:, bi8 + 4:bi8 + 5],
                               lnc[:, bi8 + 1:bi8 + 2],
                               1.0 / C, None, ALU.mult))
            var2 = P.sop("vector", mk("tensor_sub", lnc[:, bi8 + 5:bi8 + 6],
                                      lnc[:, bi8 + 4:bi8 + 5],
                                      lnc[:, bi8 + 3:bi8 + 4]))
            P.sop("scalar", mk("activation", lnc[:, bi8 + 6:bi8 + 7],
                               lnc[:, bi8 + 5:bi8 + 6], AF.Ln,
                               bias=epsc[:, 2:3]), wait=[("sDV", var2)])
            rstd2 = P.sop("scalar", mk("activation", lnc[:, bi8 + 7:bi8 + 8],
                                       lnc[:, bi8 + 6:bi8 + 7],
                                       AF.Exp, scale=-0.5))
            w = [("sSC", rstd2)]
            if t >= 2:
                w.append(("sOUT", 16 * (t - 1)))
            norm_done_t = P.sop("vector", mk("tensor_scalar",
                                             yt[:, C * (t % 2):C * (t % 2) + C],
                                             bfr,
                                             lnc[:, bi8 + 2:bi8 + 3],
                                             lnc[:, bi8 + 7:bi8 + 8],
                                             ALU.subtract, ALU.mult), wait=w)
            norm_done[t] = norm_done_t
            P.op("sync", mk("dma_start", out=y_d[128 * t:128 * t + 128, :],
                            in_=yt[:, C * (t % 2):C * (t % 2) + C]),
                 wait=[("sDV", norm_done[t])],
                 inc=(("sOUT", 16),))

        for tp in range(3):
            p3b_load(tp, [])
        for i in range(NT + 1):
            if 0 <= i - 1:
                p3b_fin(i - 1)
            if i < NT:
                if i + 3 < NT:
                    p3b_load(i + 3,
                             [("sDV", norm_done[i - 1])] if i >= 1 else [])
                p3b_stats(i)
        P.op("sync", mk("nop"), wait=[("sOUT", 16 * NT)])
        P.flush()

    es.close()
    return nc


def _host_prep(inputs):
    f32 = np.float32
    perm = np.concatenate([np.arange(0, D, 2), np.arange(1, D, 2)])
    Wqkv = np.asarray(inputs["L0_attn_w_qkv"], f32)
    bqkv = np.asarray(inputs["L0_attn_b_qkv"], f32)
    WT = np.ascontiguousarray(Wqkv.T)
    Wp = WT.copy()
    bp = bqkv.copy()
    for h in range(H):
        for base in (0, C):
            cols = base + h * D + perm
            Wp[:, base + h * D:base + (h + 1) * D] = WT[:, cols]
            bp[base + h * D:base + (h + 1) * D] = bqkv[cols]
    gq = np.asarray(inputs["L0_attn_q_gamma"], f32)[:, perm]
    gk = np.asarray(inputs["L0_attn_k_gamma"], f32)[:, perm]
    gq_cols = np.empty((128, NP), f32)
    gk_cols = np.empty((128, NP), f32)
    for j in range(NP):
        gq_cols[0:64, j] = gq[2 * j]
        gq_cols[64:128, j] = gq[2 * j + 1]
        gk_cols[0:64, j] = gk[2 * j]
        gk_cols[64:128, j] = gk[2 * j + 1]
    half = D // 2
    inv_freq = (1.0 / (10000.0 ** (np.arange(half, dtype=f32) / f32(half)))).astype(f32)
    ang = (np.arange(L, dtype=f32)[:, None] * inv_freq[None, :]).astype(f32)
    cosr = np.tile(np.cos(ang).astype(f32), (1, H))
    sinr = np.tile(np.sin(ang).astype(f32), (1, H))

    shared = dict(
        wqkvT=np.ascontiguousarray(Wp),
        bqkv=np.ascontiguousarray(bp[None, :]),
        woutT=np.ascontiguousarray(np.asarray(inputs["L0_attn_w_out"], f32).T),
        bout=np.ascontiguousarray(np.asarray(inputs["L0_attn_b_out"], f32)[None, :]),
        gq=gq_cols, gk=gk_cols, cosr=cosr, sinr=sinr,
        cw1=np.ascontiguousarray(np.asarray(inputs["L0_ffn_cn_w1"], f32)),
        cw2=np.ascontiguousarray(np.asarray(inputs["L0_ffn_cn_w2"], f32)),
        cw3=np.ascontiguousarray(np.asarray(inputs["L0_ffn_cn_w3"], f32)),
        cw4=np.ascontiguousarray(np.asarray(inputs["L0_ffn_cn_w4"], f32)),
        cb1=np.ascontiguousarray(np.asarray(inputs["L0_ffn_cn_b1"], f32).reshape(8, 128).T),
        cb2=np.ascontiguousarray(np.asarray(inputs["L0_ffn_cn_b2"], f32).reshape(4, 128).T),
        cb3=np.ascontiguousarray(np.asarray(inputs["L0_ffn_cn_b3"], f32).reshape(2, 128).T),
        cb4=np.asarray(inputs["L0_ffn_cn_b4"], f32).reshape(1, 1),
        gw1=np.ascontiguousarray(np.asarray(inputs["L0_ffn_gc_w1"], f32)),
        gw2=np.ascontiguousarray(np.asarray(inputs["L0_ffn_gc_w2"], f32)),
        gw3=np.ascontiguousarray(np.asarray(inputs["L0_ffn_gc_w3"], f32)),
        gb1=np.ascontiguousarray(np.asarray(inputs["L0_ffn_gc_b1"], f32).reshape(4, 128).T),
        gb2=np.ascontiguousarray(np.asarray(inputs["L0_ffn_gc_b2"], f32).reshape(2, 128).T),
        gb3=np.asarray(inputs["L0_ffn_gc_b3"], f32).reshape(1, 1),
        ar=np.asarray(inputs["L0_ffn_adaptive_range"], f32).reshape(1, 1),
    )
    x = np.asarray(inputs["L0_attn_x"], f32)
    return shared, x


def kernel(**inputs):
    shared, x = _host_prep(inputs)
    nc = build_program()
    in_maps = []
    for b in range(8):
        m = dict(shared)
        m["x"] = np.ascontiguousarray(x[b])
        in_maps.append(m)
    res = run_bass_kernel_spmd(nc, in_maps, core_ids=list(range(8)))
    out = np.stack([np.asarray(res.results[b]["y"]) for b in range(8)], axis=0)
    return out.astype(np.float32)
